# version 01
# baseline (speedup 1.0000x reference)
import sys
sys.path.insert(0, '/opt/trn_rl_repo')
from contextlib import ExitStack

import numpy as np
import ml_dtypes

import concourse.bass as bass
import concourse.mybir as mybir
from concourse.bass_utils import run_bass_kernel_spmd

F32 = mybir.dt.float32
BF16 = mybir.dt.bfloat16
I32 = mybir.dt.int32
AF = mybir.ActivationFunctionType
ALU = mybir.AluOpType
AXX = mybir.AxisListType.X
BF = ml_dtypes.bfloat16

B = 2
T = 8192
D = 2048
BLK = 512
NBLK = 16
H = 16
DK = 128
CS = 64
FH0 = 5632
FH1 = 16384
NCORE = 8
PROV = (3, 7, 11, 15)
NOFF = [4 * p for p in PROV]
NEG = -1.0e30
PI = float(np.pi)
C1 = 6.28125
C2 = 2.0 * PI - C1

WSPECS = {
    "q0": ("L0_attn_Wq", D, D, "plain"),
    "k0": ("L0_attn_Wk", D, D, "plain"),
    "v0": ("L0_attn_Wv", D, D, "plain"),
    "g0": ("L0_attn_Wg", D, D, "plain"),
    "o0": ("L0_attn_Wo", D, D, "plain"),
    "w1": ("L0_ffn_w1", FH0, D, "bitnet"),
    "w2": ("L0_ffn_w2", FH0, D, "bitnet"),
    "w3": ("L0_ffn_w3", D, FH0, "bitnet"),
    "q1": ("L1_attn_Wq", D, D, "plain"),
    "k1": ("L1_attn_Wk", D, D, "plain"),
    "v1": ("L1_attn_Wv", D, D, "plain"),
    "o1": ("L1_attn_Wo", D, D, "plain"),
    "g1": ("L1_ffn_gate_W_packed", FH1, D, "nvfp4"),
    "u1": ("L1_ffn_up_W_packed", FH1, D, "nvfp4"),
    "d1": ("L1_ffn_down_W_packed", D, FH1, "nvfp4"),
}
SCKEY = {"g1": "L1_ffn_gate_scales", "u1": "L1_ffn_up_scales", "d1": "L1_ffn_down_scales"}
BIT_IDX = {"w1": 0, "w2": 1, "w3": 2}
WNAMES = list(WSPECS.keys())


def core_blocks(j):
    return [j, 7 - j, 8 + j, 15 - j]


def ag_pos(b):
    for r in range(4):
        for s, bb in enumerate(core_blocks(r)):
            if bb == b:
                return (r * 4 + s) * BLK
    raise AssertionError


def stot_row(b):
    for r in range(4):
        for s, bb in enumerate(core_blocks(r)):
            if bb == b:
                return r * 4 + s
    raise AssertionError


def host_meta(c):
    g, j = divmod(c, 4)
    blocks = core_blocks(j)
    tok = np.concatenate([np.arange(b * BLK, (b + 1) * BLK) for b in blocks])
    needs = [j + 1, 8 - j, 9 + j, 16 - j]
    return dict(g=g, j=j, blocks=blocks, tok=tok, needs=needs)


def host_consts(mc):
    tok = mc["tok"].astype(np.float64)
    i = np.arange(64, dtype=np.float64)
    inv_freq = 1.0 / (10000.0 ** (2 * i / 128.0))
    ang = tok[:, None] * inv_freq[None, :]
    rope_sin = np.sin(ang).astype(np.float32)
    rope_cos = np.cos(ang).astype(np.float32)
    pos1 = tok.astype(np.float32).reshape(-1, 1)
    cols = []
    for s in range(4):
        n_vis = 4 * (mc["needs"][s] - 1)
        for t in range(NOFF[s]):
            cols.append(0.0 if t < n_vis else NEG)
    flagcols = np.broadcast_to(np.array(cols, np.float32)[None, :], (128, 144)).copy()
    flags = []
    for s in range(4):
        b_own = mc["blocks"][s]
        for t in range(PROV[s]):
            flags.append(1.0 if t < b_own else 0.0)
    flags0 = np.broadcast_to(np.array(flags, np.float32)[None, :], (128, 36)).copy()
    iden = np.eye(128).astype(BF)
    tri = np.zeros((128, 4 * BLK), np.float32)
    kk = np.arange(128)[:, None]
    qq = np.arange(BLK)[None, :]
    for ii in range(4):
        tri[:, ii * BLK:(ii + 1) * BLK] = (128 * ii + kk <= qq)
    tri = tri.astype(BF)
    invN = np.zeros((1, 8), np.float32)
    invN[0, 0] = 1.0 / (FH0 * D)
    invN[0, 1] = 1.0 / (FH0 * D)
    invN[0, 2] = 1.0 / (D * FH0)
    return dict(rope_sin=rope_sin, rope_cos=rope_cos, pos1=pos1, flagcols=flagcols,
                flags0=flags0, iden=iden, tri=tri, invN=invN)


class Buf:
    __slots__ = ("ap_", "w", "r", "name")

    def __init__(self, handle, name=""):
        self.ap_ = handle
        self.name = name
        self.w = None
        self.r = []

    def __getitem__(self, idx):
        return self.ap_[idx]


class Ctx:
    ENGS = ("pe", "dve", "act", "pl", "sp")

    def __init__(self, nc, es):
        self.nc = nc
        self.es = es
        self.t = {}
        self.sems = {}
        self.cnt = {}
        self.waited = {e: {} for e in self.ENGS}
        self.q = {e: [] for e in self.ENGS}
        self._semn = 0
        self.nops = 0

    def sem(self, name):
        if name not in self.sems:
            self.sems[name] = self.es.enter_context(self.nc.semaphore(f"s{self._semn}_{name}"))
            self._semn += 1
            self.cnt[name] = 0
        return self.sems[name]

    def sbuf(self, name, shape, dtype, es=None):
        return Buf((es or self.es).enter_context(self.nc.sbuf_tensor(name, shape, dtype)), name)

    def psum(self, name, shape, dtype, es=None):
        assert (shape[1] * (4 if dtype == F32 else 2)) % 2048 == 0, (name, shape)
        return Buf((es or self.es).enter_context(self.nc.psum_tensor(name, shape, dtype)), name)

    def constcol(self, value):
        key = f"cc_{value}"
        if not hasattr(self, "_ccache"):
            self._ccache = {}
        if key not in self._ccache:
            b = self.sbuf(f"ccol{len(self._ccache)}", [128, 1], F32)
            self.op("dve", lambda e, b=b, value=value: e.memset(b.ap_[:, :], value), writes=[b])
            self._ccache[key] = b
        return self._ccache[key]

    def dram(self, name, shape, dtype, kind="Internal"):
        h = self.nc.dram_tensor(name, shape, dtype, kind=kind)
        self.t[name] = Buf(h.ap(), name)
        return self.t[name]

    def op(self, engname, fn, reads=(), writes=(), dma=False, sem=None, tag=True):
        waits = []
        wcache = self.waited[engname]

        def need(tok):
            if tok is None:
                return
            sn, sv = tok
            if wcache.get(sn, 0) < sv:
                waits.append((sn, sv))
                wcache[sn] = sv

        for b in reads:
            need(b.w)
        for b in writes:
            need(b.w)
            for tok in b.r:
                need(tok)
        if sem is not None:
            semname = sem
        elif dma:
            semname = "d_" + (writes[0].name if writes else engname)
        else:
            semname = engname
        inc = 16 if dma else 1
        tok = None
        if tag:
            self.sem(semname)
            self.cnt[semname] = self.cnt.get(semname, 0) + inc
            tok = (semname, self.cnt[semname])
            for b in writes:
                b.w = tok
                b.r = []
            for b in reads:
                b.r.append(tok)
        semmap = self.sems
        self.nops += 1

        def emit(eng, waits=waits, fn=fn, tok=tok, inc=inc):
            for (sn, sv) in waits:
                eng.wait_ge(semmap[sn], sv)
            inst = fn(eng)
            if tok is not None:
                inst.then_inc(semmap[tok[0]], inc)
            return inst

        self.q[engname].append(emit)
        return tok

    def final_wait(self, engname):
        semmap = self.sems
        items = [(sn, sv) for sn, sv in self.cnt.items() if sv > 0]

        def emit(eng, items=items):
            for sn, sv in items:
                eng.wait_ge(semmap[sn], sv)
        self.q[engname].append(emit)

    def flush(self):
        qs = {e: q for e, q in self.q.items() if q}
        if not qs:
            return
        self.q = {e: [] for e in self.ENGS}
        with self.nc.Block() as blk:
            if "sp" in qs:
                @blk.sync
                def _(e, lst=qs["sp"]):
                    for cl in lst:
                        cl(e)
            if "pe" in qs:
                @blk.tensor
                def _(e, lst=qs["pe"]):
                    for cl in lst:
                        cl(e)
            if "dve" in qs:
                @blk.vector
                def _(e, lst=qs["dve"]):
                    for cl in lst:
                        cl(e)
            if "act" in qs:
                @blk.scalar
                def _(e, lst=qs["act"]):
                    for cl in lst:
                        cl(e)
            if "pl" in qs:
                @blk.gpsimd
                def _(e, lst=qs["pl"]):
                    for cl in lst:
                        cl(e)


class Launch:
    def __init__(self, name):
        self.name = name
        self.nc = bass.Bass("TRN2", target_bir_lowering=False)
        self.es = ExitStack()
        self.ctx = Ctx(self.nc, self.es)

    def run(self, in_maps):
        self.ctx.flush()
        try:
            res = run_bass_kernel_spmd(self.nc, in_maps, core_ids=list(range(NCORE)))
        finally:
            self.es.close()
        return res.results



def TAP(buf, off, dims):
    h = buf.ap_
    if isinstance(h, bass.AP):
        h = h.tensor
    return bass.AP(h, off, dims)

def dma(ctx, dst, src, reads=(), writes=(), eng="sp", transpose=False):
    def fn(e, dst=dst, src=src, transpose=transpose):
        return e.dma_start(out=dst, in_=src, transpose=transpose)
    return ctx.op(eng, fn, reads=reads, writes=writes, dma=True)


class Rot:
    def __init__(self, bufs):
        self.bufs = list(bufs)
        self.i = -1

    def next(self):
        self.i = (self.i + 1) % len(self.bufs)
        return self.bufs[self.i]


def emit_prep(ctx, fused):
    ctx.constcol(0.5); ctx.constcol(1e-5)
    t = ctx.t
    with ExitStack() as pes:
        sb_in = ctx.sbuf("p0_in", [128, 2048], F32, pes)
        sb_pk = ctx.sbuf("p0_pk", [128, 1024], I32, pes)
        sb_sc = ctx.sbuf("p0_sc", [128, 64], F32, pes)
        sb_bf = ctx.sbuf("p0_bf", [128, 2048], BF16, pes)
        sb_red = ctx.sbuf("p0_red", [128, 8], F32, pes)
        sb_gam = ctx.sbuf("p0_gam", [128, 8], F32, pes)
        sb_tau = ctx.sbuf("p0_tau", [128, 8], F32, pes)
        sb_row = ctx.sbuf("p0_row", [1, 8], F32, pes)
        sb_invn = ctx.sbuf("p0_invn", [1, 8], F32, pes)
        sb_iden = ctx.sbuf("p0_iden", [128, 128], BF16, pes)
        sb_ones = ctx.sbuf("p0_ones", [128, 8], F32, pes)
        sb_onesr = ctx.sbuf("p0_onesr", [1, 128], F32, pes)
        sb_tp = Rot([ctx.sbuf(f"p0_tpo{i}", [128, 128], BF16, pes) for i in range(4)])
        si_a = ctx.sbuf("p0_ia", [128, 1024], I32, pes)
        si_b = ctx.sbuf("p0_ib", [128, 1024], I32, pes)
        si_c = ctx.sbuf("p0_ic", [128, 1024], I32, pes)
        sf_j = ctx.sbuf("p0_j", [128, 1024], F32, pes)
        sf_c = ctx.sbuf("p0_c", [128, 1024], F32, pes)
        sf_s = ctx.sbuf("p0_s", [128, 1024], F32, pes)
        sf_m = ctx.sbuf("p0_m", [128, 1024], F32, pes)
        ps_t = Rot([ctx.psum(f"p0_pt{i}", [128, 1024], BF16, pes) for i in range(2)])
        ps_g = ctx.psum("p0_pg", [128, 512], F32, pes)

        dma(ctx, sb_iden[:, :], t["c_iden"][:, :], reads=[t["c_iden"]], writes=[sb_iden])
        dma(ctx, sb_invn[:, :], t["c_invN"][:, :], reads=[t["c_invN"]], writes=[sb_invn])
        ctx.op("dve", lambda e: e.memset(sb_gam.ap_[:, :], 0.0), writes=[sb_gam])
        ctx.op("dve", lambda e: e.memset(sb_ones.ap_[:, :], 1.0), writes=[sb_ones])
        ctx.op("dve", lambda e: e.memset(sb_onesr.ap_[:, :], 1.0), writes=[sb_onesr])

        for mi, wname in enumerate(("w1", "w2", "w3")):
            key, O, F, _k = WSPECS[wname]
            fsl = F // 8
            src = t[f"wsh_{wname}"]
            for oi in range(0, O, 128):
                dma(ctx, sb_in[:, 0:fsl], src[oi:oi + 128, :], reads=[src], writes=[sb_in])
                ctx.op("dve", lambda e, fsl=fsl: e.reduce_sum(
                    sb_red.ap_[:, 0:1], sb_in.ap_[:, 0:fsl], AXX, apply_absolute_value=True),
                    reads=[sb_in], writes=[sb_red])
                ctx.op("dve", lambda e, mi=mi: e.tensor_tensor(
                    sb_gam.ap_[:, mi:mi + 1], sb_gam.ap_[:, mi:mi + 1], sb_red.ap_[:, 0:1], ALU.add),
                    reads=[sb_red, sb_gam], writes=[sb_gam])
        dma(ctx, t["gam_in"].ap_, sb_gam[:, 0:4], reads=[sb_gam], writes=[t["gam_in"]])
        ctx.op("pl", lambda e: e.collective_compute(
            "AllReduce", ALU.add, replica_groups=[list(range(NCORE))],
            ins=[t["gam_in"].ap_.opt()], outs=[t["gam_out"].ap_.opt()]),
            reads=[t["gam_in"]], writes=[t["gam_out"]], sem="cc_gam")
        dma(ctx, sb_gam[:, 0:4], t["gam_out"].ap_, reads=[t["gam_out"]], writes=[sb_gam])
        ctx.op("pe", lambda e: e.matmul(ps_g.ap_[0:1, 0:4], sb_ones.ap_[:, 0:1],
                                        sb_gam.ap_[:, 0:4], start=True, stop=True),
               reads=[sb_ones, sb_gam], writes=[ps_g])
        ctx.op("dve", lambda e: e.tensor_tensor(sb_row.ap_[0:1, 0:4], ps_g.ap_[0:1, 0:4],
                                                sb_invn.ap_[0:1, 0:4], ALU.mult),
               reads=[ps_g, sb_invn], writes=[sb_row])
        ctx.op("dve", lambda e: e.tensor_scalar(sb_row.ap_[0:1, 0:4], sb_row.ap_[0:1, 0:4],
                                                1e-8, 0.5, ALU.add, ALU.mult),
               reads=[sb_row], writes=[sb_row])
        ctx.op("pe", lambda e: e.matmul(ps_g.ap_[:, 4:8], sb_onesr.ap_[0:1, :],
                                        sb_row.ap_[0:1, 0:4], start=True, stop=True),
               reads=[sb_onesr, sb_row], writes=[ps_g])
        ctx.op("dve", lambda e: e.tensor_copy(sb_tau.ap_[:, 0:4], ps_g.ap_[:, 4:8]),
               reads=[ps_g], writes=[sb_tau])
        ctx.flush()

        for wname in WNAMES:
            key, O, F, kind = WSPECS[wname]
            fsl = F // 8
            wt_shard = t[f"wtsh_{wname}"]
            for oi in range(0, O, 128):
                if kind == "plain":
                    src = t[f"wsh_{wname}"]
                    dma(ctx, sb_in[:, 0:fsl], src[oi:oi + 128, :], reads=[src], writes=[sb_in])
                    ctx.op("act", lambda e, fsl=fsl: e.copy(sb_bf.ap_[:, 0:fsl], sb_in.ap_[:, 0:fsl]),
                           reads=[sb_in], writes=[sb_bf])
                elif kind == "bitnet":
                    src = t[f"wsh_{wname}"]
                    mi = BIT_IDX[wname]
                    dma(ctx, sb_in[:, 0:fsl], src[oi:oi + 128, :], reads=[src], writes=[sb_in])
                    ctx.op("dve", lambda e, fsl=fsl, mi=mi: e.tensor_scalar(
                        sf_j.ap_[:, 0:fsl], sb_in.ap_[:, 0:fsl], sb_tau.ap_[:, mi:mi + 1], None, ALU.is_gt),
                        reads=[sb_in, sb_tau], writes=[sf_j])
                    ctx.op("dve", lambda e, fsl=fsl: e.tensor_scalar_mul(
                        sf_c.ap_[:, 0:fsl], sb_in.ap_[:, 0:fsl], -1.0),
                        reads=[sb_in], writes=[sf_c])
                    ctx.op("dve", lambda e, fsl=fsl, mi=mi: e.tensor_scalar(
                        sf_c.ap_[:, 0:fsl], sf_c.ap_[:, 0:fsl], sb_tau.ap_[:, mi:mi + 1], None, ALU.is_gt),
                        reads=[sf_c, sb_tau], writes=[sf_c])
                    ctx.op("dve", lambda e, fsl=fsl: e.tensor_sub(
                        sb_bf.ap_[:, 0:fsl], sf_j.ap_[:, 0:fsl], sf_c.ap_[:, 0:fsl]),
                        reads=[sf_j, sf_c], writes=[sb_bf])
                else:
                    src = t[f"wsh_{wname}"]
                    ssc = t[f"scsh_{wname}"]
                    nb = fsl // 2
                    nblk32 = fsl // 32
                    dma(ctx, sb_pk[:, 0:nb], src[oi:oi + 128, :], reads=[src], writes=[sb_pk])
                    dma(ctx, sb_sc[:, 0:nblk32], ssc[oi:oi + 128, :], reads=[ssc], writes=[sb_sc])
                    for half in range(2):
                        if half == 0:
                            ctx.op("dve", lambda e, nb=nb: e.tensor_single_scalar(
                                si_a.ap_[:, 0:nb], sb_pk.ap_[:, 0:nb], 15, ALU.bitwise_and),
                                reads=[sb_pk], writes=[si_a])
                        else:
                            ctx.op("dve", lambda e, nb=nb: e.tensor_single_scalar(
                                si_a.ap_[:, 0:nb], sb_pk.ap_[:, 0:nb], 4, ALU.arith_shift_right),
                                reads=[sb_pk], writes=[si_a])
                        ctx.op("dve", lambda e, nb=nb: e.tensor_single_scalar(
                            si_b.ap_[:, 0:nb], si_a.ap_[:, 0:nb], 7, ALU.bitwise_and),
                            reads=[si_a], writes=[si_b])
                        ctx.op("dve", lambda e, nb=nb: e.tensor_single_scalar(
                            si_c.ap_[:, 0:nb], si_a.ap_[:, 0:nb], 3, ALU.arith_shift_right),
                            reads=[si_a], writes=[si_c])
                        ctx.op("dve", lambda e, nb=nb: e.tensor_copy(
                            sf_j.ap_[:, 0:nb], si_b.ap_[:, 0:nb]), reads=[si_b], writes=[sf_j])
                        ctx.op("dve", lambda e, nb=nb: e.tensor_copy(
                            sf_s.ap_[:, 0:nb], si_c.ap_[:, 0:nb]), reads=[si_c], writes=[sf_s])
                        ctx.op("dve", lambda e, nb=nb: e.tensor_scalar(
                            si_b.ap_[:, 0:nb], si_b.ap_[:, 0:nb], 4194304, 1056964608, ALU.mult, ALU.add),
                            reads=[si_b], writes=[si_b])
                        cb = ctx.constcol(0.5)
                        ctx.op("act", lambda e, nb=nb, cb=cb: e.activation(
                            sf_c.ap_[:, 0:nb], sf_j.ap_[:, 0:nb], AF.Relu, bias=cb.ap_[:, 0:1],
                            scale=-0.25), reads=[sf_j, cb], writes=[sf_c])
                        ctx.op("dve", lambda e, nb=nb: e.tensor_sub(
                            sf_m.ap_[:, 0:nb], si_b.ap_[:, 0:nb].bitcast(F32), sf_c.ap_[:, 0:nb]),
                            reads=[si_b, sf_c], writes=[sf_m])
                        ctx.op("dve", lambda e, nb=nb: e.tensor_scalar(
                            sf_s.ap_[:, 0:nb], sf_s.ap_[:, 0:nb], -2.0, 1.0, ALU.mult, ALU.add),
                            reads=[sf_s], writes=[sf_s])
                        ctx.op("dve", lambda e, nb=nb: e.tensor_mul(
                            sf_m.ap_[:, 0:nb], sf_m.ap_[:, 0:nb], sf_s.ap_[:, 0:nb]),
                            reads=[sf_m, sf_s], writes=[sf_m])
                        ctx.op("dve", lambda e, nb=nb, nblk32=nblk32, half=half: e.tensor_tensor(
                            bass.AP(sb_bf.ap_, half, [[2048, 128], [32, nblk32], [2, 16]]),
                            bass.AP(sf_m.ap_, 0, [[1024, 128], [16, nblk32], [1, 16]]),
                            bass.AP(sb_sc.ap_, 0, [[64, 128], [1, nblk32], [0, 16]]),
                            ALU.mult),
                            reads=[sf_m, sb_sc], writes=[sb_bf])
                for bi in range((fsl + 127) // 128):
                    bs = bi * 128
                    bn = min(128, fsl - bs)
                    pt = ps_t.next()
                    so = sb_tp.next()
                    ctx.op("pe", lambda e, bs=bs, bn=bn, pt=pt: e.transpose(
                        pt.ap_[0:bn, 0:128], sb_bf.ap_[:, bs:bs + bn], sb_iden.ap_[:, :]),
                        reads=[sb_bf, sb_iden], writes=[pt])
                    ctx.op("dve", lambda e, bn=bn, pt=pt, so=so: e.tensor_copy(
                        so.ap_[0:bn, :], pt.ap_[0:bn, 0:128]), reads=[pt], writes=[so])
                    dma(ctx, wt_shard.ap_[bs:bs + bn, oi:oi + 128],
                        so[0:bn, :], reads=[so], writes=[wt_shard], eng="pl")
        ctx.flush()
        if fused:
            for wname in WNAMES:
                ctx.op("pl", lambda e, wname=wname: e.collective_compute(
                    "AllGather", ALU.bypass, replica_groups=[list(range(NCORE))],
                    ins=[ctx.t[f"wtsh_{wname}"].ap_.opt()], outs=[ctx.t[f"wt_{wname}"].ap_.opt()]),
                    reads=[ctx.t[f"wtsh_{wname}"]], writes=[ctx.t[f"wt_{wname}"]], sem="cc")
        ctx.flush()


def load_bias_mat(ctx, pes, name, bias_t, ps_pool, n=D):
    row = ctx.sbuf(f"{name}_row", [1, n], F32, pes)
    ones = ctx.sbuf(f"{name}_one", [1, 128], F32, pes)
    mat = ctx.sbuf(f"{name}_mat", [128, n], F32, pes)
    dma(ctx, row[0:1, :], bias_t.ap_[0:1, :], reads=[bias_t], writes=[row])
    ctx.op("dve", lambda e: e.memset(ones.ap_[:, :], 1.0), writes=[ones])
    for ci in range(0, n, 512):
        ps = ps_pool.next()
        ctx.op("pe", lambda e, ci=ci, ps=ps: e.matmul(
            ps.ap_[:, 0:512], ones.ap_[0:1, :], row.ap_[0:1, ci:ci + 512],
            start=True, stop=True), reads=[ones, row], writes=[ps])
        ctx.op("dve", lambda e, ci=ci, ps=ps: e.tensor_copy(mat.ap_[:, ci:ci + 512], ps.ap_[:, 0:512]),
               reads=[ps], writes=[mat])
    return mat


def emit_ln_tile(ctx, xsum, scr, st1, st2, out_f32, out_bf, n=D):
    ctx.op("dve", lambda e: e.reduce_sum(st1.ap_[:, 0:1], xsum.ap_[:, 0:n], AXX),
           reads=[xsum], writes=[st1])
    ctx.op("act", lambda e: e.activation(scr.ap_[:, 0:n], xsum.ap_[:, 0:n], AF.Square,
                                         accum_out=st1.ap_[:, 1:2]),
           reads=[xsum], writes=[scr, st1])
    ctx.op("dve", lambda e: e.tensor_scalar_mul(st1.ap_[:, 2:4], st1.ap_[:, 0:2], 1.0 / n),
           reads=[st1], writes=[st1])
    ctx.op("dve", lambda e: e.tensor_tensor(st2.ap_[:, 0:1], st1.ap_[:, 2:3], st1.ap_[:, 2:3], ALU.mult),
           reads=[st1], writes=[st2])
    ctx.op("dve", lambda e: e.tensor_sub(st2.ap_[:, 1:2], st1.ap_[:, 3:4], st2.ap_[:, 0:1]),
           reads=[st1, st2], writes=[st2])
    cb5 = ctx.constcol(1e-5)
    ctx.op("act", lambda e, cb5=cb5: e.activation(st2.ap_[:, 2:3], st2.ap_[:, 1:2], AF.Ln,
                                                  bias=cb5.ap_[:, 0:1]),
           reads=[st2, cb5], writes=[st2])
    ctx.op("act", lambda e: e.activation(st2.ap_[:, 3:4], st2.ap_[:, 2:3], AF.Exp, scale=-0.5),
           reads=[st2], writes=[st2])
    ctx.op("dve", lambda e: e.tensor_scalar(out_f32.ap_[:, 0:n], xsum.ap_[:, 0:n],
                                            st1.ap_[:, 2:3], st2.ap_[:, 3:4],
                                            ALU.subtract, ALU.mult),
           reads=[xsum, st1, st2], writes=[out_f32])
    if out_bf is not None:
        ctx.op("dve", lambda e: e.tensor_copy(out_bf.ap_[:, 0:n], out_f32.ap_[:, 0:n]),
               reads=[out_f32], writes=[out_bf])


def load_xfm(ctx, xbf_t, xfm, row0, ntok):
    for fc in range(16):
        dma(ctx, xfm[:, fc, 0:ntok],
            xbf_t.ap_[row0:row0 + ntok, fc * 128:(fc + 1) * 128],
            reads=[xbf_t], writes=[xfm], transpose=True)


def emit_p1(ctx):
    ctx.constcol(0.5); ctx.constcol(1e-5)
    t = ctx.t
    with ExitStack() as pes:
        xin = Rot([ctx.sbuf(f"p1_xi{i}", [128, 2048], F32, pes) for i in range(2)])
        xbo = Rot([ctx.sbuf(f"p1_xo{i}", [128, 2048], BF16, pes) for i in range(2)])
        for tc in range(16):
            xi = xin.next()
            xo = xbo.next()
            dma(ctx, xi[:, :], t["x"].ap_[tc * 128:(tc + 1) * 128, :], reads=[t["x"]], writes=[xi])
            ctx.op("dve", lambda e, xi=xi, xo=xo: e.tensor_copy(xo.ap_[:, :], xi.ap_[:, :]),
                   reads=[xi], writes=[xo])
            dma(ctx, t["xbf"].ap_[tc * 128:(tc + 1) * 128, :], xo[:, :],
                reads=[xo], writes=[t["xbf"]], eng="pl")
        ctx.flush()

        xfm = ctx.sbuf("p1_xfm", [128, 16, 512], BF16, pes)
        wbuf = ctx.sbuf("p1_wb", [128, 16, 512], BF16, pes)
        wgbuf = ctx.sbuf("p1_wgb", [128, 16, 128], BF16, pes)
        kblk = ctx.sbuf("p1_kblk", [128, 4, 2048], BF16, pes)
        vblk = ctx.sbuf("p1_vblk", [128, 4, 2048], BF16, pes)
        qpiece = Rot([ctx.sbuf(f"p1_qp{i}", [128, 512], F32, pes) for i in range(2)])
        rp1 = ctx.sbuf("p1_rp1", [128, 512], F32, pes)
        rp2 = ctx.sbuf("p1_rp2", [128, 512], F32, pes)
        rp3 = ctx.sbuf("p1_rp3", [128, 512], F32, pes)
        obf = Rot([ctx.sbuf(f"p1_ob{i}", [128, 512], BF16, pes) for i in range(2)])
        sincos = ctx.sbuf("p1_sc", [128, 4, 128], F32, pes)
        onesb = ctx.sbuf("p1_onesb", [128, 1], BF16, pes)
        iden1 = ctx.sbuf("p1_iden", [128, 128], BF16, pes)
        gf32 = ctx.sbuf("p1_gf32", [128, 512], F32, pes)
        bgcols = ctx.sbuf("p1_bg", [128, 16], F32, pes)
        tpo = Rot([ctx.sbuf(f"p1_tp{i}", [128, 128], BF16, pes) for i in range(4)])
        ps = Rot([ctx.psum(f"p1_ps{i}", [128, 512], F32, pes) for i in range(2)])
        psg = Rot([ctx.psum(f"p1_pg{i}", [128, 512], F32, pes) for i in range(2)])
        pst = Rot([ctx.psum(f"p1_pt{i}", [128, 1024], BF16, pes) for i in range(2)])
        pss = ctx.psum("p1_pss", [128, 512], F32, pes)
        psb = Rot([ctx.psum("p1_psb", [128, 512], F32, pes)])

        ctx.op("dve", lambda e: e.memset(onesb.ap_[:, :], 1.0), writes=[onesb])
        dma(ctx, iden1[:, :], t["c_iden"][:, :], reads=[t["c_iden"]], writes=[iden1])
        for gc in range(16):
            dma(ctx, bgcols[:, gc:gc + 1], TAP(t["bg"], gc * 128, [[1, 128], [1, 1]]),
                reads=[t["bg"]], writes=[bgcols])
        bias = {}
        for pname in ("bq", "bk", "bv"):
            bias[pname] = load_bias_mat(ctx, pes, f"p1_{pname}", t[pname], psb)
        ctx.flush()

        for b in range(4):
            row0 = b * 512
            load_xfm(ctx, t["xbf"], xfm, row0, 512)
            for tc4 in range(4):
                dma(ctx, sincos[:, tc4, 0:64],
                    t["c_rsin"].ap_[row0 + tc4 * 128: row0 + (tc4 + 1) * 128, :],
                    reads=[t["c_rsin"]], writes=[sincos])
                dma(ctx, sincos[:, tc4, 64:128],
                    t["c_rcos"].ap_[row0 + tc4 * 128: row0 + (tc4 + 1) * 128, :],
                    reads=[t["c_rcos"]], writes=[sincos])
            for pi_, pname in enumerate(("q0", "k0", "v0")):
                bmat = bias[("bq", "bk", "bv")[pi_]]
                for ot in range(4):
                    dma(ctx, wbuf[:, :, :],
                        TAP(t[f"wt_{pname}"], ot * 512, [[2048, 128], [128 * 2048, 16], [1, 512]]),
                        reads=[t[f"wt_{pname}"]], writes=[wbuf])
                    for tc4 in range(4):
                        p = ps.next()
                        for fc in range(16):
                            ctx.op("pe", lambda e, fc=fc, tc4=tc4, p=p: e.matmul(
                                p.ap_[:, :], xfm.ap_[:, fc, tc4 * 128:(tc4 + 1) * 128],
                                wbuf.ap_[:, fc, :], start=(fc == 0), stop=(fc == 15)),
                                reads=[xfm, wbuf], writes=[p])
                        if pname == "v0":
                            ob = obf.next()
                            ctx.op("dve", lambda e, p=p, ot=ot, ob=ob, bmat=bmat: e.tensor_tensor(
                                ob.ap_[:, :], p.ap_[:, :], bmat.ap_[:, ot * 512:(ot + 1) * 512], ALU.add),
                                reads=[p, bmat], writes=[ob])
                            ctx.op("dve", lambda e, ob=ob, tc4=tc4, ot=ot: e.tensor_copy(
                                vblk.ap_[:, tc4, ot * 512:(ot + 1) * 512], ob.ap_[:, :]),
                                reads=[ob], writes=[vblk])
                            dma(ctx, t["v0tok"].ap_[row0 + tc4 * 128:row0 + (tc4 + 1) * 128,
                                                    ot * 512:(ot + 1) * 512],
                                ob[:, :], reads=[ob], writes=[t["v0tok"]], eng="pl")
                        else:
                            qp = qpiece.next()
                            ctx.op("dve", lambda e, p=p, ot=ot, qp=qp, bmat=bmat: e.tensor_tensor(
                                qp.ap_[:, :], p.ap_[:, :], bmat.ap_[:, ot * 512:(ot + 1) * 512], ALU.add),
                                reads=[p, bmat], writes=[qp])
                            if pname == "q0" and "q0pre" in t:
                                dma(ctx, t["q0pre"].ap_[row0 + tc4 * 128:row0 + (tc4 + 1) * 128,
                                                        ot * 512:(ot + 1) * 512],
                                    qp[:, :], reads=[qp], writes=[t["q0pre"]], eng="pl")
                            qe = bass.AP(qp.ap_, 0, [[512, 128], [128, 4], [2, 64]])
                            qo = bass.AP(qp.ap_, 1, [[512, 128], [128, 4], [2, 64]])
                            sn = bass.AP(sincos.ap_, tc4 * 128, [[512, 128], [0, 4], [1, 64]])
                            cs = bass.AP(sincos.ap_, tc4 * 128 + 64, [[512, 128], [0, 4], [1, 64]])
                            r1e = bass.AP(rp1.ap_, 0, [[512, 128], [128, 4], [2, 64]])
                            r1o = bass.AP(rp1.ap_, 1, [[512, 128], [128, 4], [2, 64]])
                            r2e = bass.AP(rp2.ap_, 0, [[512, 128], [128, 4], [2, 64]])
                            r2o = bass.AP(rp2.ap_, 1, [[512, 128], [128, 4], [2, 64]])
                            ctx.op("dve", lambda e, qe=qe, cs=cs, r1e=r1e: e.tensor_tensor(
                                r1e, qe, cs, ALU.mult), reads=[qp, sincos], writes=[rp1])
                            ctx.op("dve", lambda e, qo=qo, sn=sn, r2e=r2e: e.tensor_tensor(
                                r2e, qo, sn, ALU.mult), reads=[qp, sincos], writes=[rp2])
                            ctx.op("dve", lambda e, qe=qe, sn=sn, r1o=r1o: e.tensor_tensor(
                                r1o, qe, sn, ALU.mult), reads=[qp, sincos], writes=[rp1])
                            ctx.op("dve", lambda e, qo=qo, cs=cs, r2o=r2o: e.tensor_tensor(
                                r2o, qo, cs, ALU.mult), reads=[qp, sincos], writes=[rp2])
                            r3e = bass.AP(rp3.ap_, 0, [[512, 128], [128, 4], [2, 64]])
                            r3o = bass.AP(rp3.ap_, 1, [[512, 128], [128, 4], [2, 64]])
                            ctx.op("dve", lambda e, r1e=r1e, r2e=r2e, r3e=r3e: e.tensor_tensor(
                                r3e, r1e, r2e, ALU.subtract), reads=[rp1, rp2], writes=[rp3])
                            ctx.op("dve", lambda e, r1o=r1o, r2o=r2o, r3o=r3o: e.tensor_tensor(
                                r3o, r1o, r2o, ALU.add), reads=[rp1, rp2], writes=[rp3])
                            ctx.op("dve", lambda e: e.tensor_scalar_min(
                                rp1.ap_[:, :], rp3.ap_[:, :], 0.0), reads=[rp3], writes=[rp1])
                            ctx.op("act", lambda e: e.activation(
                                rp2.ap_[:, :], rp1.ap_[:, :], AF.Exp), reads=[rp1], writes=[rp2])
                            ctx.op("dve", lambda e: e.tensor_scalar_max(
                                rp1.ap_[:, :], rp3.ap_[:, :], 0.0), reads=[rp3], writes=[rp1])
                            ob = obf.next()
                            ctx.op("dve", lambda e, ob=ob: e.tensor_add(
                                ob.ap_[:, :], rp1.ap_[:, :], rp2.ap_[:, :]),
                                reads=[rp1, rp2], writes=[ob])
                            if pname == "k0":
                                ctx.op("dve", lambda e, ob=ob, tc4=tc4, ot=ot: e.tensor_copy(
                                    kblk.ap_[:, tc4, ot * 512:(ot + 1) * 512], ob.ap_[:, :]),
                                    reads=[ob], writes=[kblk])
                                dma(ctx, t["k0tok"].ap_[row0 + tc4 * 128:row0 + (tc4 + 1) * 128,
                                                        ot * 512:(ot + 1) * 512],
                                    ob[:, :], reads=[ob], writes=[t["k0tok"]], eng="pl")
                            else:
                                for oti in range(4):
                                    pt = pst.next()
                                    so = tpo.next()
                                    ctx.op("pe", lambda e, ob=ob, oti=oti, pt=pt: e.transpose(
                                        pt.ap_[:, 0:128], ob.ap_[:, oti * 128:(oti + 1) * 128],
                                        iden1.ap_[:, :]), reads=[ob, iden1], writes=[pt])
                                    ctx.op("dve", lambda e, pt=pt, so=so: e.tensor_copy(
                                        so.ap_[:, :], pt.ap_[:, 0:128]), reads=[pt], writes=[so])
                                    dma(ctx, t["q0fm"].ap_[ot * 512 + oti * 128:ot * 512 + (oti + 1) * 128,
                                                           row0 + tc4 * 128:row0 + (tc4 + 1) * 128],
                                        so[:, :], reads=[so], writes=[t["q0fm"]], eng="pl")
            for gc in range(16):
                dma(ctx, wgbuf[:, :, :],
                    TAP(t["wt_g0"], gc * 128, [[2048, 128], [128 * 2048, 16], [1, 128]]),
                    reads=[t["wt_g0"]], writes=[wgbuf])
                pg = psg.next()
                for fc in range(16):
                    ctx.op("pe", lambda e, fc=fc, pg=pg: e.matmul(
                        pg.ap_[:, :], wgbuf.ap_[:, fc, :], xfm.ap_[:, fc, 0:512],
                        start=(fc == 0), stop=(fc == 15)),
                        reads=[wgbuf, xfm], writes=[pg])
                ctx.op("act", lambda e, pg=pg, gc=gc: e.activation(
                    gf32.ap_[:, :], pg.ap_[:, :],
                    AF.Sigmoid, bias=bgcols.ap_[:, gc:gc + 1]),
                    reads=[pg, bgcols], writes=[gf32])
                dma(ctx, t["g0fm"].ap_[gc * 128:(gc + 1) * 128, row0:row0 + 512],
                    gf32[:, :], reads=[gf32], writes=[t["g0fm"]], eng="pl")
            for h in range(H):
                for tc4 in range(4):
                    ctx.op("pe", lambda e, h=h, tc4=tc4: e.matmul(
                        pss.ap_[:, 0:128], kblk.ap_[:, tc4, h * 128:(h + 1) * 128],
                        vblk.ap_[:, tc4, h * 128:(h + 1) * 128],
                        start=(tc4 == 0), stop=(tc4 == 3)),
                        reads=[kblk, vblk], writes=[pss])
                for tc4 in range(4):
                    ctx.op("pe", lambda e, h=h, tc4=tc4: e.matmul(
                        pss.ap_[:, 128:129], kblk.ap_[:, tc4, h * 128:(h + 1) * 128],
                        onesb.ap_[:, 0:1], start=(tc4 == 0), stop=(tc4 == 3)),
                        reads=[kblk, onesb], writes=[pss])
                stout = qpiece.next()
                ctx.op("dve", lambda e, stout=stout: e.tensor_copy(
                    stout.ap_[:, 0:129], pss.ap_[:, 0:129]), reads=[pss], writes=[stout])
                dma(ctx, TAP(t["stot"], b * (H * 128 * 129) + h * (128 * 129),
                                 [[129, 128], [1, 129]]),
                    stout[:, 0:129], reads=[stout], writes=[t["stot"]], eng="pl")
        ctx.flush()


def emit_p3(ctx):
    ctx.constcol(0.5); ctx.constcol(1e-5)
    t = ctx.t
    with ExitStack() as pes:
        S_all = ctx.sbuf("p3_S", [128, 16, 129], F32, pes)
        S_bf = ctx.sbuf("p3_Sbf", [128, 16, 129], BF16, pes)
        S_st = ctx.sbuf("p3_Sst", [128, 16, 129], F32, pes)
        flags = ctx.sbuf("p3_fl", [128, 36], F32, pes)
        qfm = ctx.sbuf("p3_qfm", [128, 16, 512], BF16, pes)
        gfm = ctx.sbuf("p3_gfm", [128, 16, 512], F32, pes)
        kblk = ctx.sbuf("p3_kblk", [128, 4, 2048], BF16, pes)
        vblk = ctx.sbuf("p3_vblk", [128, 4, 2048], BF16, pes)
        aout = ctx.sbuf("p3_ao", [128, 16, 512], BF16, pes)
        rrow = ctx.sbuf("p3_rr", [1, 1024], F32, pes)
        rmat = Rot([ctx.sbuf(f"p3_rm{i}", [128, 256], F32, pes) for i in range(2)])
        o1 = Rot([ctx.sbuf(f"p3_o1{i}", [128, 256], F32, pes) for i in range(2)])
        onesb = ctx.sbuf("p3_onesb", [128, 1], BF16, pes)
        onesf = ctx.sbuf("p3_onesf", [1, 128], F32, pes)
        wobuf = ctx.sbuf("p3_wo", [128, 16, 512], BF16, pes)
        xprev = ctx.sbuf("p3_xp", [128, 2048], F32, pes)
        xsum = ctx.sbuf("p3_xs", [128, 2048], F32, pes)
        xout = ctx.sbuf("p3_xo", [128, 2048], F32, pes)
        xobf = ctx.sbuf("p3_xob", [128, 2048], BF16, pes)
        scr = ctx.sbuf("p3_scr", [128, 2048], BF16, pes)
        st1 = ctx.sbuf("p3_st1", [128, 8], F32, pes)
        st2 = ctx.sbuf("p3_st2", [128, 8], F32, pes)
        ps_kv = ctx.psum("p3_kv", [128, 1024], F32, pes)
        ps_num = Rot([ctx.psum(f"p3_nm{i}", [128, 512], F32, pes) for i in range(2)])
        ps_den = ctx.psum("p3_dn", [128, 1024], F32, pes)
        ps_r = Rot([ctx.psum(f"p3_r{i}", [128, 512], F32, pes) for i in range(2)])
        ps_o = ps_r

        ctx.op("dve", lambda e: e.memset(onesb.ap_[:, :], 1.0), writes=[onesb])
        ctx.op("dve", lambda e: e.memset(onesf.ap_[:, :], 1.0), writes=[onesf])
        dma(ctx, flags[:, :], t["c_flags0"][:, :], reads=[t["c_flags0"]], writes=[flags])
        bo_mat = load_bias_mat(ctx, pes, "p3_bo", t["bo0"], ps_r)
        ctx.flush()

        for s in range(4):
            row0 = s * 512
            for fc in range(16):
                dma(ctx, qfm[:, fc, :], t["q0fm"].ap_[fc * 128:(fc + 1) * 128, row0:row0 + 512],
                    reads=[t["q0fm"]], writes=[qfm])
                dma(ctx, gfm[:, fc, :], t["g0fm"].ap_[fc * 128:(fc + 1) * 128, row0:row0 + 512],
                    reads=[t["g0fm"]], writes=[gfm])
            for tc4 in range(4):
                dma(ctx, kblk[:, tc4, :], t["k0tok"].ap_[row0 + tc4 * 128:row0 + (tc4 + 1) * 128, :],
                    reads=[t["k0tok"]], writes=[kblk])
                dma(ctx, vblk[:, tc4, :], t["v0tok"].ap_[row0 + tc4 * 128:row0 + (tc4 + 1) * 128, :],
                    reads=[t["v0tok"]], writes=[vblk])
            ctx.op("dve", lambda e: e.memset(S_all.ap_[:, :, :], 0.0), writes=[S_all])
            fbase = sum(PROV[:s])
            for tt in range(PROV[s]):
                r = stot_row(tt)
                dma(ctx, S_st[:, :, :],
                    TAP(t["stot_all"], r * (H * 128 * 129),
                            [[129, 128], [128 * 129, 16], [1, 129]]),
                    reads=[t["stot_all"]], writes=[S_st])
                ctx.op("dve", lambda e, tt=tt, fbase=fbase: e.scalar_tensor_tensor(
                    S_all.ap_[:, :, :], S_st.ap_[:, :, :], flags.ap_[:, fbase + tt:fbase + tt + 1],
                    S_all.ap_[:, :, :], ALU.mult, ALU.add),
                    reads=[S_st, flags, S_all], writes=[S_all])
            for c in range(8):
                tc4, half = divmod(c, 2)
                pbase = half * 64
                for w in range(4):
                    pkv = ps_kv
                    for hh in range(4):
                        h = w * 4 + hh
                        ctx.op("pe", lambda e, h=h, hh=hh, tc4=tc4, pbase=pbase, pkv=pkv: e.matmul(
                            pkv.ap_[:, hh * 256:hh * 256 + 128],
                            kblk.ap_[pbase:pbase + 64, tc4, h * 128:(h + 1) * 128],
                            vblk.ap_[pbase:pbase + 64, tc4, h * 128:(h + 1) * 128],
                            start=True, stop=True),
                            reads=[kblk, vblk], writes=[pkv])
                        ctx.op("pe", lambda e, h=h, hh=hh, tc4=tc4, pbase=pbase, pkv=pkv: e.matmul(
                            pkv.ap_[:, hh * 256 + 128:hh * 256 + 129],
                            kblk.ap_[pbase:pbase + 64, tc4, h * 128:(h + 1) * 128],
                            onesb.ap_[pbase:pbase + 64, 0:1],
                            start=True, stop=True),
                            reads=[kblk, onesb], writes=[pkv])
                    ctx.op("dve", lambda e, w=w, pkv=pkv: e.tensor_tensor(
                        bass.AP(S_all.ap_, w * 4 * 129, [[16 * 129, 128], [129, 4], [1, 129]]),
                        bass.AP(S_all.ap_, w * 4 * 129, [[16 * 129, 128], [129, 4], [1, 129]]),
                        bass.AP(pkv.ap_, 0, [[1024, 128], [256, 4], [1, 129]]),
                        ALU.add), reads=[pkv, S_all], writes=[S_all])
                    ctx.op("dve", lambda e, w=w: e.tensor_copy(
                        bass.AP(S_bf.ap_, w * 4 * 129, [[16 * 129, 128], [129, 4], [1, 129]]),
                        bass.AP(S_all.ap_, w * 4 * 129, [[16 * 129, 128], [129, 4], [1, 129]])),
                        reads=[S_all], writes=[S_bf])
                    pnum = ps_num.bufs[w // 2]
                    for hh in range(4):
                        h = w * 4 + hh
                        col = (w % 2) * 256 + hh * 64
                        ctx.op("pe", lambda e, h=h, col=col, c=c, pnum=pnum: e.matmul(
                            pnum.ap_[:, col:col + 64], S_bf.ap_[:, h, 0:128],
                            qfm.ap_[:, h, c * 64:(c + 1) * 64], start=True, stop=True),
                            reads=[S_bf, qfm], writes=[pnum])
                        ctx.op("pe", lambda e, h=h, c=c: e.matmul(
                            ps_den.ap_[0:1, h * 64:(h + 1) * 64], S_bf.ap_[:, h, 128:129],
                            qfm.ap_[:, h, c * 64:(c + 1) * 64], start=True, stop=True),
                            reads=[S_bf, qfm], writes=[ps_den])
                ctx.op("dve", lambda e: e.tensor_scalar_add(rrow.ap_[0:1, :], ps_den.ap_[0:1, :], 1e-6),
                       reads=[ps_den], writes=[rrow])
                ctx.op("dve", lambda e: e.reciprocal(rrow.ap_[0:1, :], rrow.ap_[0:1, :]),
                       reads=[rrow], writes=[rrow])
                for w2 in range(4):
                    pr = ps_r.next()
                    ctx.op("pe", lambda e, w2=w2, pr=pr: e.matmul(
                        pr.ap_[:, 0:256], onesf.ap_[0:1, :], rrow.ap_[0:1, w2 * 256:(w2 + 1) * 256],
                        start=True, stop=True), reads=[onesf, rrow], writes=[pr])
                    rm = rmat.next()
                    ctx.op("dve", lambda e, pr=pr, rm=rm: e.tensor_copy(rm.ap_[:, :], pr.ap_[:, 0:256]),
                           reads=[pr], writes=[rm])
                    pnum = ps_num.bufs[w2 // 2]
                    oo = o1.next()
                    ctx.op("dve", lambda e, w2=w2, rm=rm, oo=oo, pnum=pnum: e.tensor_tensor(
                        oo.ap_[:, :], pnum.ap_[:, (w2 % 2) * 256:(w2 % 2) * 256 + 256],
                        rm.ap_[:, :], ALU.mult),
                        reads=[pnum, rm], writes=[oo])
                    ctx.op("dve", lambda e, w2=w2, c=c, oo=oo: e.tensor_tensor(
                        bass.AP(aout.ap_, (w2 * 4) * 512 + c * 64, [[16 * 512, 128], [512, 4], [1, 64]]),
                        oo.ap_[:, :].rearrange("p (h t) -> p h t", h=4) if False else
                        bass.AP(oo.ap_, 0, [[256, 128], [64, 4], [1, 64]]),
                        bass.AP(gfm.ap_, (w2 * 4) * 512 + c * 64, [[16 * 512, 128], [512, 4], [1, 64]]),
                        ALU.mult),
                        reads=[oo, gfm], writes=[aout])
            for tc4 in range(4):
                trow = row0 + tc4 * 128
                dma(ctx, xprev[:, :], t["x"].ap_[trow:trow + 128, :], reads=[t["x"]], writes=[xprev])
                ctx.op("dve", lambda e: e.tensor_add(xprev.ap_[:, :], xprev.ap_[:, :], bo_mat.ap_[:, :]),
                       reads=[bo_mat, xprev], writes=[xprev])
                for ot in range(4):
                    dma(ctx, wobuf[:, :, :],
                        TAP(t["wt_o0"], ot * 512, [[2048, 128], [128 * 2048, 16], [1, 512]]),
                        reads=[t["wt_o0"]], writes=[wobuf])
                    po = ps_o.next()
                    for fc in range(16):
                        ctx.op("pe", lambda e, fc=fc, tc4=tc4, po=po: e.matmul(
                            po.ap_[:, :], aout.ap_[:, fc, tc4 * 128:(tc4 + 1) * 128],
                            wobuf.ap_[:, fc, :], start=(fc == 0), stop=(fc == 15)),
                            reads=[aout, wobuf], writes=[po])
                    ctx.op("dve", lambda e, ot=ot, po=po: e.tensor_tensor(
                        xsum.ap_[:, ot * 512:(ot + 1) * 512], po.ap_[:, :],
                        xprev.ap_[:, ot * 512:(ot + 1) * 512], ALU.add),
                        reads=[po, xprev], writes=[xsum])
                emit_ln_tile(ctx, xsum, scr, st1, st2, xout, xobf)
                dma(ctx, t["x1"].ap_[trow:trow + 128, :], xout[:, :], reads=[xout],
                    writes=[t["x1"]], eng="pl")
                dma(ctx, t["x1bf"].ap_[trow:trow + 128, :], xobf[:, :], reads=[xobf],
                    writes=[t["x1bf"]], eng="pl")
        ctx.flush()


def emit_ffn(ctx, pfx, xin_f32, xin_bf, xout_f32, xout_bf, wg, wu, wd, FH, act_kind):
    ctx.constcol(0.5); ctx.constcol(1e-5)
    """x [2048,2048]; hidden FH; act 'sig' (sigmoid(x1)*x2) or 'gelu' (gelu(g)*u)."""
    t = ctx.t
    HC = FH // 128
    halves = [range(0, HC)] if HC <= 64 else [range(0, 64), range(64, HC)]
    HMAX = max(len(hv) for hv in halves)
    af = AF.Sigmoid if act_kind == "sig" else AF.Gelu_apprx_tanh
    with ExitStack() as pes:
        xfm = ctx.sbuf(f"{pfx}_xfm", [128, 16, 512], BF16, pes)
        hid = ctx.sbuf(f"{pfx}_hid", [128, HMAX, 512], BF16, pes)
        wgb = Rot([ctx.sbuf(f"{pfx}_wg{i}", [128, 16, 128], BF16, pes) for i in range(2)])
        wub = Rot([ctx.sbuf(f"{pfx}_wu{i}", [128, 16, 128], BF16, pes) for i in range(2)])
        wdb = Rot([ctx.sbuf(f"{pfx}_wd{i}", [128, 512], BF16, pes) for i in range(3)])
        gl = ctx.sbuf(f"{pfx}_gl", [128, 512], BF16, pes)
        xacc = ctx.sbuf(f"{pfx}_xa", [128, 4, 2048], F32, pes)
        xout = ctx.sbuf(f"{pfx}_xo", [128, 2048], F32, pes)
        xobf = ctx.sbuf(f"{pfx}_xob", [128, 2048], BF16, pes)
        scr = ctx.sbuf(f"{pfx}_scr", [128, 2048], BF16, pes)
        st1 = ctx.sbuf(f"{pfx}_st1", [128, 8], F32, pes)
        st2 = ctx.sbuf(f"{pfx}_st2", [128, 8], F32, pes)
        psg = Rot([ctx.psum(f"{pfx}_pg{i}", [128, 512], F32, pes) for i in range(2)])
        psu = Rot([ctx.psum(f"{pfx}_pu{i}", [128, 512], F32, pes) for i in range(2)])
        psd = [ctx.psum(f"{pfx}_pd{i}", [128, 512], F32, pes) for i in range(4)]

        for tq in range(4):
            trow = tq * 512
            load_xfm(ctx, xin_bf, xfm, trow, 512)
            for tc4 in range(4):
                dma(ctx, xacc[:, tc4, :], xin_f32.ap_[trow + tc4 * 128:trow + (tc4 + 1) * 128, :],
                    reads=[xin_f32], writes=[xacc])
            for hv in halves:
                nh = len(hv)
                for hi, hc in enumerate(hv):
                    wgt = wgb.next()
                    wut = wub.next()
                    dma(ctx, wgt[:, :, :],
                        TAP(wg, hc * 128, [[FH, 128], [128 * FH, 16], [1, 128]]),
                        reads=[wg], writes=[wgt])
                    dma(ctx, wut[:, :, :],
                        TAP(wu, hc * 128, [[FH, 128], [128 * FH, 16], [1, 128]]),
                        reads=[wu], writes=[wut])
                    pg = psg.next()
                    pu = psu.next()
                    for fc in range(16):
                        ctx.op("pe", lambda e, fc=fc, pg=pg, wgt=wgt: e.matmul(
                            pg.ap_[:, :], wgt.ap_[:, fc, :], xfm.ap_[:, fc, :],
                            start=(fc == 0), stop=(fc == 15)), reads=[wgt, xfm], writes=[pg])
                    for fc in range(16):
                        ctx.op("pe", lambda e, fc=fc, pu=pu, wut=wut: e.matmul(
                            pu.ap_[:, :], wut.ap_[:, fc, :], xfm.ap_[:, fc, :],
                            start=(fc == 0), stop=(fc == 15)), reads=[wut, xfm], writes=[pu])
                    ctx.op("act", lambda e, pg=pg: e.activation(gl.ap_[:, :], pg.ap_[:, :], af),
                           reads=[pg], writes=[gl])
                    ctx.op("dve", lambda e, hi=hi, pu=pu: e.tensor_tensor(
                        hid.ap_[:, hi, :], gl.ap_[:, :], pu.ap_[:, :], ALU.mult),
                        reads=[gl, pu], writes=[hid])
                extra = [psg.bufs[0], psg.bufs[1], psu.bufs[0], psu.bufs[1]]
                for opass in range(2):
                    tiles = [(tc4, opass * 2 + oi) for tc4 in range(4) for oi in range(2)]
                    pmap = {}
                    for idx, (tc4, ot) in enumerate(tiles):
                        pmap[(tc4, ot)] = (psd + extra)[idx]
                    for hi, hc in enumerate(hv):
                        for oi in range(2):
                            ot = opass * 2 + oi
                            wdt = wdb.next()
                            dma(ctx, wdt[:, :],
                                wd.ap_[hc * 128:(hc + 1) * 128, ot * 512:(ot + 1) * 512],
                                reads=[wd], writes=[wdt])
                            for tc4 in range(4):
                                p = pmap[(tc4, ot)]
                                ctx.op("pe", lambda e, hi=hi, tc4=tc4, p=p, wdt=wdt, nh=nh: e.matmul(
                                    p.ap_[:, :], hid.ap_[:, hi, tc4 * 128:(tc4 + 1) * 128],
                                    wdt.ap_[:, :], start=(hi == 0), stop=(hi == nh - 1)),
                                    reads=[hid, wdt], writes=[p])
                    for (tc4, ot) in tiles:
                        p = pmap[(tc4, ot)]
                        ctx.op("dve", lambda e, tc4=tc4, ot=ot, p=p: e.tensor_tensor(
                            xacc.ap_[:, tc4, ot * 512:(ot + 1) * 512], p.ap_[:, :],
                            xacc.ap_[:, tc4, ot * 512:(ot + 1) * 512], ALU.add),
                            reads=[p, xacc], writes=[xacc])
            for tc4 in range(4):
                ctx.op("dve", lambda e, tc4=tc4: e.reduce_sum(
                    st1.ap_[:, 0:1], xacc.ap_[:, tc4, :], AXX), reads=[xacc], writes=[st1])
                ctx.op("act", lambda e, tc4=tc4: e.activation(
                    scr.ap_[:, :], xacc.ap_[:, tc4, :], AF.Square, accum_out=st1.ap_[:, 1:2]),
                    reads=[xacc], writes=[scr, st1])
                ctx.op("dve", lambda e: e.tensor_scalar_mul(st1.ap_[:, 2:4], st1.ap_[:, 0:2], 1.0 / D),
                       reads=[st1], writes=[st1])
                ctx.op("dve", lambda e: e.tensor_tensor(st2.ap_[:, 0:1], st1.ap_[:, 2:3],
                                                        st1.ap_[:, 2:3], ALU.mult),
                       reads=[st1], writes=[st2])
                ctx.op("dve", lambda e: e.tensor_sub(st2.ap_[:, 1:2], st1.ap_[:, 3:4], st2.ap_[:, 0:1]),
                       reads=[st1, st2], writes=[st2])
                cb5 = ctx.constcol(1e-5)
                ctx.op("act", lambda e, cb5=cb5: e.activation(st2.ap_[:, 2:3], st2.ap_[:, 1:2],
                                                              AF.Ln, bias=cb5.ap_[:, 0:1]),
                       reads=[st2, cb5], writes=[st2])
                ctx.op("act", lambda e: e.activation(st2.ap_[:, 3:4], st2.ap_[:, 2:3], AF.Exp, scale=-0.5),
                       reads=[st2], writes=[st2])
                ctx.op("dve", lambda e, tc4=tc4: e.tensor_scalar(
                    xout.ap_[:, :], xacc.ap_[:, tc4, :], st1.ap_[:, 2:3], st2.ap_[:, 3:4],
                    ALU.subtract, ALU.mult), reads=[xacc, st1, st2], writes=[xout])
                dma(ctx, xout_f32.ap_[trow + tc4 * 128:trow + (tc4 + 1) * 128, :], xout[:, :],
                    reads=[xout], writes=[xout_f32], eng="pl")
                if xout_bf is not None:
                    ctx.op("dve", lambda e: e.tensor_copy(xobf.ap_[:, :], xout.ap_[:, :]),
                           reads=[xout], writes=[xobf])
                    dma(ctx, xout_bf.ap_[trow + tc4 * 128:trow + (tc4 + 1) * 128, :], xobf[:, :],
                        reads=[xobf], writes=[xout_bf], eng="pl")
        ctx.flush()


def emit_p5(ctx):
    ctx.constcol(0.5); ctx.constcol(1e-5)
    t = ctx.t
    with ExitStack() as pes:
        xfm = ctx.sbuf("p5_xfm", [128, 16, 512], BF16, pes)
        wbuf = ctx.sbuf("p5_wb", [128, 16, 512], BF16, pes)
        qpiece = Rot([ctx.sbuf(f"p5_qp{i}", [128, 512], F32, pes) for i in range(2)])
        rp1 = ctx.sbuf("p5_rp1", [128, 512], F32, pes)
        rp2 = ctx.sbuf("p5_rp2", [128, 512], F32, pes)
        obf = Rot([ctx.sbuf(f"p5_ob{i}", [128, 512], BF16, pes) for i in range(2)])
        thm = ctx.sbuf("p5_thm", [128, 64], F32, pes)
        throw = ctx.sbuf("p5_thr", [1, 64], F32, pes)
        onesf = ctx.sbuf("p5_onef", [1, 128], F32, pes)
        postile = ctx.sbuf("p5_pos", [128, 1], F32, pes)
        freq = ctx.sbuf("p5_fq", [128, 64], F32, pes)
        freq2 = ctx.sbuf("p5_fq2", [128, 64], F32, pes)
        kint = ctx.sbuf("p5_ki", [128, 64], I32, pes)
        kf = ctx.sbuf("p5_kf", [128, 64], F32, pes)
        sincos = ctx.sbuf("p5_sc", [128, 128], F32, pes)
        ps = Rot([ctx.psum(f"p5_ps{i}", [128, 512], F32, pes) for i in range(2)])
        psb = Rot([ctx.psum("p5_psb", [128, 512], F32, pes)])

        ctx.op("dve", lambda e: e.memset(onesf.ap_[:, :], 1.0), writes=[onesf])
        dma(ctx, throw[0:1, :], t["theta"].ap_[0:1, :], reads=[t["theta"]], writes=[throw])
        pb = psb.next()
        ctx.op("pe", lambda e, pb=pb: e.matmul(pb.ap_[:, 0:64], onesf.ap_[0:1, :],
                                               throw.ap_[0:1, :], start=True, stop=True),
               reads=[onesf, throw], writes=[pb])
        ctx.op("dve", lambda e, pb=pb: e.tensor_copy(thm.ap_[:, :], pb.ap_[:, 0:64]),
               reads=[pb], writes=[thm])
        ctx.flush()

        for tc in range(16):
            trow = tc * 128
            dma(ctx, postile[:, :], t["c_pos1"].ap_[trow:trow + 128, :],
                reads=[t["c_pos1"]], writes=[postile])
            ctx.op("dve", lambda e: e.tensor_scalar(freq.ap_[:, :], thm.ap_[:, :],
                                                    postile.ap_[:, 0:1], None, ALU.mult),
                   reads=[thm, postile], writes=[freq])
            ctx.op("dve", lambda e: e.tensor_scalar_add(freq2.ap_[:, :], freq.ap_[:, :], PI / 2),
                   reads=[freq], writes=[freq2])
            for half, fq in ((0, freq), (1, freq2)):
                ctx.op("dve", lambda e, fq=fq: e.tensor_scalar_mul(
                    kf.ap_[:, :], fq.ap_[:, :], 1.0 / (2 * PI)),
                    reads=[fq], writes=[kf])
                ctx.op("dve", lambda e: e.tensor_copy(kint.ap_[:, :], kf.ap_[:, :]),
                       reads=[kf], writes=[kint])
                ctx.op("dve", lambda e: e.tensor_copy(kf.ap_[:, :], kint.ap_[:, :]),
                       reads=[kint], writes=[kf])
                ctx.op("dve", lambda e, fq=fq: e.scalar_tensor_tensor(
                    fq.ap_[:, :], kf.ap_[:, :], -C1, fq.ap_[:, :], ALU.mult, ALU.add),
                    reads=[kf, fq], writes=[fq])
                ctx.op("dve", lambda e, fq=fq: e.scalar_tensor_tensor(
                    fq.ap_[:, :], kf.ap_[:, :], -C2, fq.ap_[:, :], ALU.mult, ALU.add),
                    reads=[kf, fq], writes=[fq])
                ctx.op("dve", lambda e, fq=fq: e.tensor_scalar(
                    fq.ap_[:, :], fq.ap_[:, :], PI, -PI, ALU.min, ALU.max),
                    reads=[fq], writes=[fq])
                ctx.op("act", lambda e, fq=fq, half=half: e.activation(
                    sincos.ap_[:, half * 64:half * 64 + 64], fq.ap_[:, :], AF.Sin),
                    reads=[fq], writes=[sincos])
            if tc % 4 == 0:
                load_xfm(ctx, t["x2bf"], xfm, (tc // 4) * 512, 512)
            for pname, outt in (("q1", "q1tok"), ("k1", "k1tok"), ("v1", "v1tok")):
                for ot in range(4):
                    if tc % 4 == 0:
                        pass
                    dma(ctx, wbuf[:, :, :],
                        TAP(t[f"wt_{pname}"], ot * 512,
                                [[2048, 128], [128 * 2048, 16], [1, 512]]),
                        reads=[t[f"wt_{pname}"]], writes=[wbuf])
                    p = ps.next()
                    for fc in range(16):
                        ctx.op("pe", lambda e, fc=fc, tc=tc, p=p: e.matmul(
                            p.ap_[:, :], xfm.ap_[:, fc, (tc % 4) * 128:(tc % 4 + 1) * 128],
                            wbuf.ap_[:, fc, :], start=(fc == 0), stop=(fc == 15)),
                            reads=[xfm, wbuf], writes=[p])
                    ob = obf.next()
                    if pname == "v1":
                        ctx.op("dve", lambda e, p=p, ob=ob: e.tensor_copy(ob.ap_[:, :], p.ap_[:, :]),
                               reads=[p], writes=[ob])
                    else:
                        qp = qpiece.next()
                        ctx.op("dve", lambda e, p=p, qp=qp: e.tensor_copy(qp.ap_[:, :], p.ap_[:, :]),
                               reads=[p], writes=[qp])
                        qe = bass.AP(qp.ap_, 0, [[512, 128], [128, 4], [2, 64]])
                        qo = bass.AP(qp.ap_, 1, [[512, 128], [128, 4], [2, 64]])
                        sn = bass.AP(sincos.ap_, 0, [[128, 128], [0, 4], [1, 64]])
                        cs = bass.AP(sincos.ap_, 64, [[128, 128], [0, 4], [1, 64]])
                        h1 = bass.AP(rp1.ap_, 0, [[512, 128], [128, 4], [1, 64]])
                        h2 = bass.AP(rp1.ap_, 64, [[512, 128], [128, 4], [1, 64]])
                        t1 = bass.AP(rp2.ap_, 0, [[512, 128], [128, 4], [1, 64]])
                        t2 = bass.AP(rp2.ap_, 64, [[512, 128], [128, 4], [1, 64]])
                        ctx.op("dve", lambda e, qe=qe, cs=cs, h1=h1: e.tensor_tensor(
                            h1, qe, cs, ALU.mult), reads=[qp, sincos], writes=[rp1])
                        ctx.op("dve", lambda e, qo=qo, sn=sn, t1=t1: e.tensor_tensor(
                            t1, qo, sn, ALU.mult), reads=[qp, sincos], writes=[rp2])
                        ctx.op("dve", lambda e, qe=qe, sn=sn, h2=h2: e.tensor_tensor(
                            h2, qe, sn, ALU.mult), reads=[qp, sincos], writes=[rp1])
                        ctx.op("dve", lambda e, qo=qo, cs=cs, t2=t2: e.tensor_tensor(
                            t2, qo, cs, ALU.mult), reads=[qp, sincos], writes=[rp2])
                        o_lo = bass.AP(ob.ap_, 0, [[512, 128], [128, 4], [1, 64]])
                        o_hi = bass.AP(ob.ap_, 64, [[512, 128], [128, 4], [1, 64]])
                        ctx.op("dve", lambda e, h1=h1, t1=t1, o_lo=o_lo: e.tensor_tensor(
                            o_lo, h1, t1, ALU.subtract), reads=[rp1, rp2], writes=[ob])
                        ctx.op("dve", lambda e, h2=h2, t2=t2, o_hi=o_hi: e.tensor_tensor(
                            o_hi, h2, t2, ALU.add), reads=[rp1, rp2], writes=[ob])
                    dma(ctx, t[outt].ap_[trow:trow + 128, ot * 512:(ot + 1) * 512], ob[:, :],
                        reads=[ob], writes=[t[outt]], eng="pl")
        ctx.flush()


def emit_p7(ctx):
    ctx.constcol(0.5); ctx.constcol(1e-5)
    t = ctx.t
    PR = 32
    with ExitStack() as pes:
        khead = ctx.sbuf("p7_kh", [128, 16 * 512], BF16, pes)
        vhead = ctx.sbuf("p7_vh", [128, 64, 128], BF16, pes)
        qhead = ctx.sbuf("p7_qh", [128, 2048], BF16, pes)
        kdiag = ctx.sbuf("p7_kd", [128, 512], BF16, pes)
        vdiag = ctx.sbuf("p7_vd", [128, 4, 128], BF16, pes)
        sqb = ctx.sbuf("p7_sq", [128, 2048], BF16, pes)
        Pbuf = ctx.sbuf("p7_P", [128, PR * 512], BF16, pes)
        ctxall = ctx.sbuf("p7_ctx", [128, 16, 2048], BF16, pes)
        flagc = ctx.sbuf("p7_flag", [128, 144], F32, pes)
        biasb = ctx.sbuf("p7_bias", [128, 146], F32, pes)
        tric = ctx.sbuf("p7_tri", [128, 4 * 512], BF16, pes)
        onesb = ctx.sbuf("p7_onesb", [128, 1], BF16, pes)
        onesf = ctx.sbuf("p7_onesf", [1, 128], F32, pes)
        nrm = ctx.sbuf("p7_nrm", [1, 8], F32, pes)
        rrow = ctx.sbuf("p7_rr", [1, 512], F32, pes)
        rmat = ctx.sbuf("p7_rm", [128, 512], F32, pes)
        wobuf = ctx.sbuf("p7_wo", [128, 16, 512], BF16, pes)
        xprev = ctx.sbuf("p7_xp", [128, 2048], F32, pes)
        xout = ctx.sbuf("p7_xo", [128, 2048], F32, pes)
        xobf = ctx.sbuf("p7_xob", [128, 2048], BF16, pes)
        scr = ctx.sbuf("p7_scr", [128, 2048], BF16, pes)
        st1 = ctx.sbuf("p7_st1", [128, 8], F32, pes)
        st2 = ctx.sbuf("p7_st2", [128, 8], F32, pes)
        ps_sc = Rot([ctx.psum(f"p7_sc{i}", [128, 512], F32, pes) for i in range(2)])
        ps_ctx = ctx.psum("p7_ctxp", [128, 512], F32, pes)
        ps_sum = ctx.psum("p7_sump", [128, 512], F32, pes)
        ps_m = Rot([ctx.psum(f"p7_m{i}", [128, 512], F32, pes) for i in range(2)])
        ps_o = Rot([ctx.psum(f"p7_o{i}", [128, 512], F32, pes) for i in range(2)])

        ctx.op("dve", lambda e: e.memset(onesb.ap_[:, :], 1.0), writes=[onesb])
        ctx.op("dve", lambda e: e.memset(onesf.ap_[:, :], 1.0), writes=[onesf])
        dma(ctx, flagc[:, :], t["c_flagcols"][:, :], reads=[t["c_flagcols"]], writes=[flagc])
        dma(ctx, tric[:, :], t["c_tri"][:, :], reads=[t["c_tri"]], writes=[tric])
        bo_mat = load_bias_mat(ctx, pes, "p7_bo", t["bo1"], ps_o)
        ctx.flush()

        for h in range(H):
            for b in range(NBLK):
                p0 = ag_pos(b)
                dma(ctx, khead[:, b * 512:(b + 1) * 512],
                    t["kall"].ap_[p0:p0 + 512, h * 128:(h + 1) * 128],
                    reads=[t["kall"]], writes=[khead], transpose=True)
                dma(ctx, bass.AP(vhead.ap_, b * 4 * 128, [[64 * 128, 128], [128, 4], [1, 128]]),
                    TAP(t["vall"], p0 * 2048 + h * 128,
                        [[2048, 128], [128 * 2048, 4], [1, 128]]),
                    reads=[t["vall"]], writes=[vhead])
            for sq in range(4):
                dma(ctx, qhead[:, sq * 512:(sq + 1) * 512],
                    t["q1tok"].ap_[sq * 512:(sq + 1) * 512, h * 128:(h + 1) * 128],
                    reads=[t["q1tok"]], writes=[qhead], transpose=True)
            ctx.op("dve", lambda e: e.memset(nrm.ap_[:, :], 0.0), writes=[nrm])
            ctx.op("act", lambda e: e.activation(sqb.ap_[:, :], qhead.ap_[:, :], AF.Square),
                   reads=[qhead], writes=[sqb])
            for jj in range(4):
                pm = ps_m.next()
                ctx.op("pe", lambda e, jj=jj, pm=pm: e.matmul(
                    pm.ap_[0:1, 0:512], onesb.ap_[:, 0:1], sqb.ap_[:, jj * 512:(jj + 1) * 512],
                    start=True, stop=True), reads=[onesb, sqb], writes=[pm])
                ctx.op("dve", lambda e, pm=pm: e.reduce_max(
                    nrm.ap_[0:1, 2:3], pm.ap_[0:1, 0:512], AXX), reads=[pm], writes=[nrm])
                ctx.op("dve", lambda e: e.tensor_tensor(
                    nrm.ap_[0:1, 0:1], nrm.ap_[0:1, 0:1], nrm.ap_[0:1, 2:3], ALU.max),
                    reads=[nrm], writes=[nrm])
            for kc in range(4):
                ctx.op("act", lambda e, kc=kc: e.activation(
                    sqb.ap_[:, :], khead.ap_[:, kc * 2048:(kc + 1) * 2048], AF.Square),
                    reads=[khead], writes=[sqb])
                for jj in range(4):
                    pm = ps_m.next()
                    ctx.op("pe", lambda e, jj=jj, pm=pm: e.matmul(
                        pm.ap_[0:1, 0:512], onesb.ap_[:, 0:1], sqb.ap_[:, jj * 512:(jj + 1) * 512],
                        start=True, stop=True), reads=[onesb, sqb], writes=[pm])
                    ctx.op("dve", lambda e, pm=pm: e.reduce_max(
                        nrm.ap_[0:1, 2:3], pm.ap_[0:1, 0:512], AXX), reads=[pm], writes=[nrm])
                    ctx.op("dve", lambda e: e.tensor_tensor(
                        nrm.ap_[0:1, 1:2], nrm.ap_[0:1, 1:2], nrm.ap_[0:1, 2:3], ALU.max),
                        reads=[nrm], writes=[nrm])
            ctx.op("dve", lambda e: e.tensor_tensor(nrm.ap_[0:1, 3:4], nrm.ap_[0:1, 0:1],
                                                    nrm.ap_[0:1, 1:2], ALU.mult),
                   reads=[nrm], writes=[nrm])
            ctx.op("act", lambda e: e.activation(nrm.ap_[0:1, 4:5], nrm.ap_[0:1, 3:4], AF.Ln),
                   reads=[nrm], writes=[nrm])
            ctx.op("act", lambda e: e.activation(nrm.ap_[0:1, 5:6], nrm.ap_[0:1, 4:5], AF.Exp, scale=0.5),
                   reads=[nrm], writes=[nrm])
            ctx.op("dve", lambda e: e.tensor_scalar_mul(nrm.ap_[0:1, 6:7], nrm.ap_[0:1, 5:6],
                                                        -1.0 / 11.313708498984761),
                   reads=[nrm], writes=[nrm])
            pm = ps_m.next()
            ctx.op("pe", lambda e, pm=pm: e.matmul(pm.ap_[:, 0:1], onesf.ap_[0:1, :],
                                                   nrm.ap_[0:1, 6:7], start=True, stop=True),
                   reads=[onesf, nrm], writes=[pm])
            ctx.op("dve", lambda e, pm=pm: e.tensor_copy(biasb.ap_[:, 144:145], pm.ap_[:, 0:1]),
                   reads=[pm], writes=[biasb])
            ctx.op("dve", lambda e: e.tensor_tensor(
                biasb.ap_[:, 0:144], flagc.ap_[:, :],
                bass.AP(biasb.ap_, 144, [[146, 128], [0, 144]]), ALU.add),
                reads=[flagc, biasb], writes=[biasb])

            for s in range(4):
                noff = NOFF[s]
                fbase = sum(NOFF[:s])
                nch = noff + 4
                dma(ctx, kdiag[:, :], t["k1tok"].ap_[s * 512:(s + 1) * 512, h * 128:(h + 1) * 128],
                    reads=[t["k1tok"]], writes=[kdiag], transpose=True)
                dma(ctx, bass.AP(vdiag.ap_, 0, [[4 * 128, 128], [128, 4], [1, 128]]),
                    TAP(t["v1tok"], s * 512 * 2048 + h * 128,
                        [[2048, 128], [128 * 2048, 4], [1, 128]]),
                    reads=[t["v1tok"]], writes=[vdiag])
                for r0 in range(0, nch, PR):
                    r1 = min(r0 + PR, nch)
                    for tt in range(r0, r1):
                        psc = ps_sc.next()
                        pslot = tt - r0
                        if tt < noff:
                            ctx.op("pe", lambda e, tt=tt, s=s, psc=psc: e.matmul(
                                psc.ap_[:, :], khead.ap_[:, tt * 128:(tt + 1) * 128],
                                qhead.ap_[:, s * 512:(s + 1) * 512], start=True, stop=True),
                                reads=[khead, qhead], writes=[psc])
                            ctx.op("act", lambda e, pslot=pslot, psc=psc, fbase=fbase, tt=tt: e.activation(
                                Pbuf.ap_[:, pslot * 512:(pslot + 1) * 512], psc.ap_[:, :], AF.Exp,
                                bias=biasb.ap_[:, fbase + tt:fbase + tt + 1],
                                scale=1.0 / 11.313708498984761),
                                reads=[psc, biasb], writes=[Pbuf])
                        else:
                            i = tt - noff
                            ctx.op("pe", lambda e, i=i, s=s, psc=psc: e.matmul(
                                psc.ap_[:, :], kdiag.ap_[:, i * 128:(i + 1) * 128],
                                qhead.ap_[:, s * 512:(s + 1) * 512], start=True, stop=True),
                                reads=[kdiag, qhead], writes=[psc])
                            ctx.op("act", lambda e, pslot=pslot, psc=psc: e.activation(
                                Pbuf.ap_[:, pslot * 512:(pslot + 1) * 512], psc.ap_[:, :], AF.Exp,
                                bias=biasb.ap_[:, 144:145], scale=1.0 / 11.313708498984761),
                                reads=[psc, biasb], writes=[Pbuf])
                            ctx.op("dve", lambda e, pslot=pslot, i=i: e.tensor_tensor(
                                Pbuf.ap_[:, pslot * 512:(pslot + 1) * 512],
                                Pbuf.ap_[:, pslot * 512:(pslot + 1) * 512],
                                tric.ap_[:, i * 512:(i + 1) * 512], ALU.mult),
                                reads=[tric, Pbuf], writes=[Pbuf])
                    for tt in range(r0, r1):
                        pslot = tt - r0
                        ctx.op("pe", lambda e, pslot=pslot, tt=tt, nch=nch: e.matmul(
                            ps_sum.ap_[0:1, :], onesb.ap_[:, 0:1],
                            Pbuf.ap_[:, pslot * 512:(pslot + 1) * 512],
                            start=(tt == 0), stop=(tt == nch - 1)),
                            reads=[onesb, Pbuf], writes=[ps_sum])
                    for tt in range(r0, r1):
                        pslot = tt - r0
                        if tt < noff:
                            lhs = vhead.ap_[:, tt, :]
                            rd = vhead
                        else:
                            lhs = vdiag.ap_[:, tt - noff, :]
                            rd = vdiag
                        ctx.op("pe", lambda e, pslot=pslot, tt=tt, nch=nch, lhs=lhs: e.matmul(
                            ps_ctx.ap_[:, :], lhs, Pbuf.ap_[:, pslot * 512:(pslot + 1) * 512],
                            start=(tt == 0), stop=(tt == nch - 1)),
                            reads=[rd, Pbuf], writes=[ps_ctx])
                ctx.op("dve", lambda e: e.reciprocal(rrow.ap_[0:1, :], ps_sum.ap_[0:1, :]),
                       reads=[ps_sum], writes=[rrow])
                pm = ps_m.next()
                ctx.op("pe", lambda e, pm=pm: e.matmul(pm.ap_[:, :], onesf.ap_[0:1, :],
                                                       rrow.ap_[0:1, :], start=True, stop=True),
                       reads=[onesf, rrow], writes=[pm])
                ctx.op("dve", lambda e, pm=pm: e.tensor_copy(rmat.ap_[:, :], pm.ap_[:, :]),
                       reads=[pm], writes=[rmat])
                ctx.op("dve", lambda e, h=h, s=s: e.tensor_tensor(
                    ctxall.ap_[:, h, s * 512:(s + 1) * 512], ps_ctx.ap_[:, :], rmat.ap_[:, :], ALU.mult),
                    reads=[ps_ctx, rmat], writes=[ctxall])
        for tc in range(16):
            trow = tc * 128
            dma(ctx, xprev[:, :], t["x2"].ap_[trow:trow + 128, :], reads=[t["x2"]], writes=[xprev])
            ctx.op("dve", lambda e: e.tensor_add(xprev.ap_[:, :], xprev.ap_[:, :], bo_mat.ap_[:, :]),
                   reads=[bo_mat, xprev], writes=[xprev])
            for ot in range(4):
                dma(ctx, wobuf[:, :, :],
                    TAP(t["wt_o1"], ot * 512, [[2048, 128], [128 * 2048, 16], [1, 512]]),
                    reads=[t["wt_o1"]], writes=[wobuf])
                po = ps_o.next()
                for fc in range(16):
                    ctx.op("pe", lambda e, fc=fc, tc=tc, po=po: e.matmul(
                        po.ap_[:, :], ctxall.ap_[:, fc, tc * 128:tc * 128 + 128],
                        wobuf.ap_[:, fc, :], start=(fc == 0), stop=(fc == 15)),
                        reads=[ctxall, wobuf], writes=[po])
                ctx.op("dve", lambda e, ot=ot, po=po: e.tensor_tensor(
                    xprev.ap_[:, ot * 512:(ot + 1) * 512], po.ap_[:, :],
                    xprev.ap_[:, ot * 512:(ot + 1) * 512], ALU.add),
                    reads=[po, xprev], writes=[xprev])
            emit_ln_tile(ctx, xprev, scr, st1, st2, xout, xobf)
            dma(ctx, t["x3"].ap_[trow:trow + 128, :], xout[:, :], reads=[xout],
                writes=[t["x3"]], eng="pl")
            dma(ctx, t["x3bf"].ap_[trow:trow + 128, :], xobf[:, :], reads=[xobf],
                writes=[t["x3bf"]], eng="pl")
        ctx.flush()


def declare_consts(ctx, which):
    cmap = dict(c_iden=([128, 128], BF16), c_invN=([1, 8], F32),
                c_rsin=([2048, 64], F32), c_rcos=([2048, 64], F32),
                c_pos1=([2048, 1], F32), c_flagcols=([128, 144], F32),
                c_flags0=([128, 36], F32), c_tri=([128, 4 * 512], BF16))
    for n in which:
        ctx.dram(n, *cmap[n], kind="ExternalInput")


CONST_SRC = dict(c_iden="iden", c_invN="invN", c_rsin="rope_sin", c_rcos="rope_cos",
                 c_pos1="pos1", c_flagcols="flagcols", c_flags0="flags0", c_tri="tri")


def run_unfused(inputs, upto=99, capture=None):
    metas = [host_meta(c) for c in range(NCORE)]
    consts = [host_consts(m) for m in metas]
    x_full = np.asarray(inputs["L0_attn_x"])
    xs = [np.ascontiguousarray(x_full[metas[c]["g"], metas[c]["tok"], :]) for c in range(NCORE)]
    cap = capture if capture is not None else {}

    L = Launch("prep")
    ctx = L.ctx
    for wname in WNAMES:
        key, O, F, kind = WSPECS[wname]
        fsl = F // 8
        if kind == "nvfp4":
            ctx.dram(f"wsh_{wname}", [O, fsl // 2], I32, kind="ExternalInput")
            ctx.dram(f"scsh_{wname}", [O, fsl // 32], F32, kind="ExternalInput")
        else:
            ctx.dram(f"wsh_{wname}", [O, fsl], F32, kind="ExternalInput")
        ctx.dram(f"wtsh_{wname}", [fsl, O], BF16, kind="ExternalOutput")
    ctx.dram("gam_in", [128, 4], F32, kind="Internal")
    ctx.dram("gam_out", [128, 4], F32, kind="Internal")
    declare_consts(ctx, ["c_iden", "c_invN"])
    emit_prep(ctx, fused=False)
    ctx.final_wait("sp")
    in_maps = []
    for c in range(NCORE):
        m = {}
        for wname in WNAMES:
            key, O, F, kind = WSPECS[wname]
            fsl = F // 8
            if kind == "nvfp4":
                m[f"wsh_{wname}"] = np.ascontiguousarray(np.asarray(inputs[key])[:, c * (fsl // 2):(c + 1) * (fsl // 2)])
                m[f"scsh_{wname}"] = np.ascontiguousarray(np.asarray(inputs[SCKEY[wname]])[:, c * (fsl // 32):(c + 1) * (fsl // 32)])
            else:
                m[f"wsh_{wname}"] = np.ascontiguousarray(np.asarray(inputs[key])[:, c * fsl:(c + 1) * fsl])
        m["c_iden"] = consts[c]["iden"]
        m["c_invN"] = consts[c]["invN"]
        in_maps.append(m)
    r = L.run(in_maps)
    wt = {w: np.concatenate([r[c][f"wtsh_{w}"] for c in range(NCORE)], axis=0) for w in WNAMES}
    cap["wt"] = wt
    if upto <= 1:
        return None

    L = Launch("p1")
    ctx = L.ctx
    ctx.dram("x", [2048, 2048], F32, kind="ExternalInput")
    for w in ("q0", "k0", "v0", "g0"):
        ctx.dram(f"wt_{w}", [D, D], BF16, kind="ExternalInput")
    for bn in ("bq", "bk", "bv", "bg"):
        ctx.dram(bn, [1, D], F32, kind="ExternalInput")
    declare_consts(ctx, ["c_iden", "c_rsin", "c_rcos"])
    ctx.dram("xbf", [2048, 2048], BF16, kind="ExternalOutput")
    ctx.dram("q0fm", [2048, 2048], BF16, kind="ExternalOutput")
    ctx.dram("k0tok", [2048, 2048], BF16, kind="ExternalOutput")
    ctx.dram("v0tok", [2048, 2048], BF16, kind="ExternalOutput")
    ctx.dram("g0fm", [2048, 2048], F32, kind="ExternalOutput")
    ctx.dram("stot", [4, H * 128 * 129], F32, kind="ExternalOutput")
    import os as _os
    if _os.environ.get("P1DEBUG"):
        ctx.dram("q0pre", [2048, 2048], F32, kind="ExternalOutput")
    emit_p1(ctx)
    ctx.final_wait("sp")
    in_maps = []
    for c in range(NCORE):
        m = dict(x=xs[c], wt_q0=wt["q0"], wt_k0=wt["k0"], wt_v0=wt["v0"], wt_g0=wt["g0"],
                 bq=np.asarray(inputs["L0_attn_bq"]).reshape(1, -1),
                 bk=np.asarray(inputs["L0_attn_bk"]).reshape(1, -1),
                 bv=np.asarray(inputs["L0_attn_bv"]).reshape(1, -1),
                 bg=np.asarray(inputs["L0_attn_bg"]).reshape(1, -1),
                 c_iden=consts[c]["iden"], c_rsin=consts[c]["rope_sin"], c_rcos=consts[c]["rope_cos"])
        in_maps.append(m)
    r1 = L.run(in_maps)
    cap["p1"] = r1
    if upto <= 2:
        return None

    stot_all = [np.concatenate([r1[g * 4 + rr]["stot"] for rr in range(4)], axis=0) for g in range(2)]
    L = Launch("p3")
    ctx = L.ctx
    ctx.dram("x", [2048, 2048], F32, kind="ExternalInput")
    ctx.dram("q0fm", [2048, 2048], BF16, kind="ExternalInput")
    ctx.dram("g0fm", [2048, 2048], F32, kind="ExternalInput")
    ctx.dram("k0tok", [2048, 2048], BF16, kind="ExternalInput")
    ctx.dram("v0tok", [2048, 2048], BF16, kind="ExternalInput")
    ctx.dram("stot_all", [16, H * 128 * 129], F32, kind="ExternalInput")
    ctx.dram("wt_o0", [D, D], BF16, kind="ExternalInput")
    ctx.dram("bo0", [1, D], F32, kind="ExternalInput")
    declare_consts(ctx, ["c_flags0"])
    ctx.dram("x1", [2048, 2048], F32, kind="ExternalOutput")
    ctx.dram("x1bf", [2048, 2048], BF16, kind="ExternalOutput")
    emit_p3(ctx)
    ctx.final_wait("sp")
    in_maps = []
    for c in range(NCORE):
        m = dict(x=xs[c], q0fm=r1[c]["q0fm"], g0fm=r1[c]["g0fm"], k0tok=r1[c]["k0tok"],
                 v0tok=r1[c]["v0tok"], stot_all=stot_all[metas[c]["g"]],
                 wt_o0=wt["o0"], bo0=np.asarray(inputs["L0_attn_bo"]).reshape(1, -1),
                 c_flags0=consts[c]["flags0"])
        in_maps.append(m)
    r3 = L.run(in_maps)
    cap["p3"] = r3
    if upto <= 3:
        return None

    L = Launch("p4")
    ctx = L.ctx
    ctx.dram("x1", [2048, 2048], F32, kind="ExternalInput")
    ctx.dram("x1bf", [2048, 2048], BF16, kind="ExternalInput")
    for w in ("w1", "w2", "w3"):
        ctx.dram(f"wt_{w}", [WSPECS[w][2], WSPECS[w][1]], BF16, kind="ExternalInput")
    ctx.dram("x2", [2048, 2048], F32, kind="ExternalOutput")
    ctx.dram("x2bf", [2048, 2048], BF16, kind="ExternalOutput")
    emit_ffn(ctx, "p4", ctx.t["x1"], ctx.t["x1bf"], ctx.t["x2"], ctx.t["x2bf"],
             ctx.t["wt_w1"], ctx.t["wt_w2"], ctx.t["wt_w3"], FH0, "sig")
    ctx.final_wait("sp")
    in_maps = [dict(x1=r3[c]["x1"], x1bf=r3[c]["x1bf"], wt_w1=wt["w1"], wt_w2=wt["w2"],
                    wt_w3=wt["w3"]) for c in range(NCORE)]
    r4 = L.run(in_maps)
    cap["p4"] = r4
    if upto <= 4:
        return None

    L = Launch("p5")
    ctx = L.ctx
    ctx.dram("x2bf", [2048, 2048], BF16, kind="ExternalInput")
    for w in ("q1", "k1", "v1"):
        ctx.dram(f"wt_{w}", [D, D], BF16, kind="ExternalInput")
    ctx.dram("theta", [1, 64], F32, kind="ExternalInput")
    declare_consts(ctx, ["c_pos1"])
    ctx.dram("q1tok", [2048, 2048], BF16, kind="ExternalOutput")
    ctx.dram("k1tok", [2048, 2048], BF16, kind="ExternalOutput")
    ctx.dram("v1tok", [2048, 2048], BF16, kind="ExternalOutput")
    emit_p5(ctx)
    ctx.final_wait("sp")
    in_maps = [dict(x2bf=r4[c]["x2bf"], wt_q1=wt["q1"], wt_k1=wt["k1"], wt_v1=wt["v1"],
                    theta=np.asarray(inputs["L1_attn_theta"]).reshape(1, -1),
                    c_pos1=consts[c]["pos1"]) for c in range(NCORE)]
    r5 = L.run(in_maps)
    cap["p5"] = r5
    if upto <= 5:
        return None

    kall = [np.concatenate([r5[g * 4 + rr]["k1tok"] for rr in range(4)], axis=0) for g in range(2)]
    vall = [np.concatenate([r5[g * 4 + rr]["v1tok"] for rr in range(4)], axis=0) for g in range(2)]
    L = Launch("p7")
    ctx = L.ctx
    ctx.dram("x2", [2048, 2048], F32, kind="ExternalInput")
    ctx.dram("q1tok", [2048, 2048], BF16, kind="ExternalInput")
    ctx.dram("k1tok", [2048, 2048], BF16, kind="ExternalInput")
    ctx.dram("v1tok", [2048, 2048], BF16, kind="ExternalInput")
    ctx.dram("kall", [8192, 2048], BF16, kind="ExternalInput")
    ctx.dram("vall", [8192, 2048], BF16, kind="ExternalInput")
    ctx.dram("wt_o1", [D, D], BF16, kind="ExternalInput")
    ctx.dram("bo1", [1, D], F32, kind="ExternalInput")
    declare_consts(ctx, ["c_flagcols", "c_tri"])
    ctx.dram("x3", [2048, 2048], F32, kind="ExternalOutput")
    ctx.dram("x3bf", [2048, 2048], BF16, kind="ExternalOutput")
    emit_p7(ctx)
    ctx.final_wait("sp")
    in_maps = [dict(x2=r4[c]["x2"], q1tok=r5[c]["q1tok"], k1tok=r5[c]["k1tok"],
                    v1tok=r5[c]["v1tok"], kall=kall[metas[c]["g"]], vall=vall[metas[c]["g"]],
                    wt_o1=wt["o1"], bo1=np.asarray(inputs["L1_attn_bo"]).reshape(1, -1),
                    c_flagcols=consts[c]["flagcols"], c_tri=consts[c]["tri"])
               for c in range(NCORE)]
    r7 = L.run(in_maps)
    cap["p7"] = r7
    if upto <= 6:
        return None

    L = Launch("p8")
    ctx = L.ctx
    ctx.dram("x3", [2048, 2048], F32, kind="ExternalInput")
    ctx.dram("x3bf", [2048, 2048], BF16, kind="ExternalInput")
    for w in ("g1", "u1", "d1"):
        ctx.dram(f"wt_{w}", [WSPECS[w][2], WSPECS[w][1]], BF16, kind="ExternalInput")
    ctx.dram("out", [2048, 2048], F32, kind="ExternalOutput")
    emit_ffn(ctx, "p8", ctx.t["x3"], ctx.t["x3bf"], ctx.t["out"], None,
             ctx.t["wt_g1"], ctx.t["wt_u1"], ctx.t["wt_d1"], FH1, "gelu")
    ctx.final_wait("sp")
    in_maps = [dict(x3=r7[c]["x3"], x3bf=r7[c]["x3bf"], wt_g1=wt["g1"], wt_u1=wt["u1"],
                    wt_d1=wt["d1"]) for c in range(NCORE)]
    r8 = L.run(in_maps)
    cap["p8"] = r8

    out = np.zeros((B, T, D), np.float32)
    for c in range(NCORE):
        out[metas[c]["g"], metas[c]["tok"], :] = r8[c]["out"]
    return out


def kernel(**inputs):
    return run_unfused(inputs)


# version 04
# speedup vs baseline: 1.0141x; 1.0141x over previous
import sys
import re
sys.path.insert(0, '/opt/trn_rl_repo')
from contextlib import ExitStack

import numpy as np
import ml_dtypes

import concourse.bass as bass
import concourse.mybir as mybir
from concourse.bass_utils import run_bass_kernel_spmd

F32 = mybir.dt.float32
BF16 = mybir.dt.bfloat16
I32 = mybir.dt.int32
AF = mybir.ActivationFunctionType
ALU = mybir.AluOpType
AXX = mybir.AxisListType.X
BF = ml_dtypes.bfloat16

B = 2
T = 8192
D = 2048
BLK = 512
NBLK = 16
H = 16
DK = 128
CS = 64
FH0 = 5632
FH1 = 16384
NCORE = 8
PROV = (3, 7, 11, 15)
NOFF = [4 * p for p in PROV]
NEG = -1.0e30
PI = float(np.pi)
C1 = 6.28125
C2 = 2.0 * PI - C1

WSPECS = {
    "q0": ("L0_attn_Wq", D, D, "plain"),
    "k0": ("L0_attn_Wk", D, D, "plain"),
    "v0": ("L0_attn_Wv", D, D, "plain"),
    "g0": ("L0_attn_Wg", D, D, "plain"),
    "o0": ("L0_attn_Wo", D, D, "plain"),
    "w1": ("L0_ffn_w1", FH0, D, "bitnet"),
    "w2": ("L0_ffn_w2", FH0, D, "bitnet"),
    "w3": ("L0_ffn_w3", D, FH0, "bitnet"),
    "q1": ("L1_attn_Wq", D, D, "plain"),
    "k1": ("L1_attn_Wk", D, D, "plain"),
    "v1": ("L1_attn_Wv", D, D, "plain"),
    "o1": ("L1_attn_Wo", D, D, "plain"),
    "g1": ("L1_ffn_gate_W_packed", FH1, D, "nvfp4"),
    "u1": ("L1_ffn_up_W_packed", FH1, D, "nvfp4"),
    "d1": ("L1_ffn_down_W_packed", D, FH1, "nvfp4"),
}
SCKEY = {"g1": "L1_ffn_gate_scales", "u1": "L1_ffn_up_scales", "d1": "L1_ffn_down_scales"}
BIT_IDX = {"w1": 0, "w2": 1, "w3": 2}
WNAMES = list(WSPECS.keys())


def core_blocks(j):
    return [j, 7 - j, 8 + j, 15 - j]


def ag_pos(b):
    for r in range(4):
        for s, bb in enumerate(core_blocks(r)):
            if bb == b:
                return (r * 4 + s) * BLK
    raise AssertionError


def stot_row(b):
    for r in range(4):
        for s, bb in enumerate(core_blocks(r)):
            if bb == b:
                return r * 4 + s
    raise AssertionError


def host_meta(c):
    g, j = divmod(c, 4)
    blocks = core_blocks(j)
    tok = np.concatenate([np.arange(b * BLK, (b + 1) * BLK) for b in blocks])
    needs = [j + 1, 8 - j, 9 + j, 16 - j]
    return dict(g=g, j=j, blocks=blocks, tok=tok, needs=needs)


def host_consts(mc):
    tok = mc["tok"].astype(np.float64)
    i = np.arange(64, dtype=np.float64)
    inv_freq = 1.0 / (10000.0 ** (2 * i / 128.0))
    ang = tok[:, None] * inv_freq[None, :]
    rope_sin = np.sin(ang).astype(np.float32)
    rope_cos = np.cos(ang).astype(np.float32)
    pos1 = tok.astype(np.float32).reshape(-1, 1)
    cols = []
    for s in range(4):
        n_vis = 4 * (mc["needs"][s] - 1)
        for t in range(NOFF[s]):
            cols.append(0.0 if t < n_vis else NEG)
    flagcols = np.broadcast_to(np.array(cols, np.float32)[None, :], (128, 144)).copy()
    flags = []
    for s in range(4):
        b_own = mc["blocks"][s]
        for t in range(PROV[s]):
            flags.append(1.0 if t < b_own else 0.0)
    flags0 = np.broadcast_to(np.array(flags, np.float32)[None, :], (128, 36)).copy()
    iden = np.eye(128).astype(BF)
    tri = np.zeros((128, 4 * BLK), np.float32)
    kk = np.arange(128)[:, None]
    qq = np.arange(BLK)[None, :]
    for ii in range(4):
        tri[:, ii * BLK:(ii + 1) * BLK] = (128 * ii + kk <= qq)
    tri = tri.astype(BF)
    invN = np.zeros((1, 8), np.float32)
    invN[0, 0] = 1.0 / (FH0 * D)
    invN[0, 1] = 1.0 / (FH0 * D)
    invN[0, 2] = 1.0 / (D * FH0)
    return dict(rope_sin=rope_sin, rope_cos=rope_cos, pos1=pos1, flagcols=flagcols,
                flags0=flags0, iden=iden, tri=tri, invN=invN)


class Buf:
    __slots__ = ("ap_", "w", "r", "name")

    def __init__(self, handle, name=""):
        self.ap_ = handle
        self.name = name
        self.w = None
        self.r = []

    def __getitem__(self, idx):
        return self.ap_[idx]


class Ctx:
    ENGS = ("pe", "dve", "act", "pl", "sp")

    def __init__(self, nc, es):
        self.nc = nc
        self.es = es
        self.t = {}
        self.sems = {}
        self.cnt = {}
        self.waited = {e: {} for e in self.ENGS}
        self.q = {e: [] for e in self.ENGS}
        self._semn = 0
        self.nops = 0

    def sem(self, name):
        if name not in self.sems:
            self.sems[name] = self.es.enter_context(self.nc.semaphore(f"s{self._semn}_{name}"))
            self._semn += 1
            self.cnt[name] = 0
        return self.sems[name]

    def sbuf(self, name, shape, dtype, es=None):
        return Buf((es or self.es).enter_context(self.nc.sbuf_tensor(name, shape, dtype)), name)

    def psum(self, name, shape, dtype, es=None):
        assert (shape[1] * (4 if dtype == F32 else 2)) % 2048 == 0, (name, shape)
        return Buf((es or self.es).enter_context(self.nc.psum_tensor(name, shape, dtype)), name)

    def constcol(self, value):
        key = f"cc_{value}"
        if not hasattr(self, "_ccache"):
            self._ccache = {}
        if key not in self._ccache:
            b = self.sbuf(f"ccol{len(self._ccache)}", [128, 1], F32)
            self.op("dve", lambda e, b=b, value=value: e.memset(b.ap_[:, :], value), writes=[b])
            self._ccache[key] = b
        return self._ccache[key]

    def dram(self, name, shape, dtype, kind="Internal"):
        h = self.nc.dram_tensor(name, shape, dtype, kind=kind)
        self.t[name] = Buf(h.ap(), name)
        return self.t[name]

    def op(self, engname, fn, reads=(), writes=(), dma=False, sem=None, tag=True):
        waits = []
        wcache = self.waited[engname]

        def need(tok):
            if tok is None:
                return
            sn, sv = tok
            if wcache.get(sn, 0) < sv:
                waits.append((sn, sv))
                wcache[sn] = sv

        for b in reads:
            need(b.w)
        for b in writes:
            need(b.w)
            for tok in b.r:
                need(tok)
        if sem is not None:
            semname = sem
        elif dma:
            bn = writes[0].name if writes else engname
            bn = re.sub(r"^p\d+_", "", bn)
            semname = "d_" + bn
        else:
            semname = engname
        inc = 16 if dma else 1
        tok = None
        if tag:
            self.sem(semname)
            self.cnt[semname] = self.cnt.get(semname, 0) + inc
            tok = (semname, self.cnt[semname])
            for b in writes:
                b.w = tok
                b.r = []
            for b in reads:
                b.r.append(tok)
        semmap = self.sems
        self.nops += 1

        def emit(eng, waits=waits, fn=fn, tok=tok, inc=inc):
            for (sn, sv) in waits:
                eng.wait_ge(semmap[sn], sv)
            inst = fn(eng)
            if tok is not None:
                inst.then_inc(semmap[tok[0]], inc)
            return inst

        self.q[engname].append(emit)
        return tok

    def drain_dmas(self):
        semmap = self.sems
        items = [(sn, sv) for sn, sv in self.cnt.items()
                 if sv > 0 and (sn.startswith("d_") or sn.startswith("cc_"))]

        def emit(eng, items=items):
            for sn, sv in items:
                eng.wait_ge(semmap[sn], sv)
        self.q["pl"].append(emit)
        wc = self.waited["pl"]
        for sn, sv in items:
            if wc.get(sn, 0) < sv:
                wc[sn] = sv

    def final_wait(self, engname):
        semmap = self.sems
        items = [(sn, sv) for sn, sv in self.cnt.items() if sv > 0]

        def emit(eng, items=items):
            for sn, sv in items:
                eng.wait_ge(semmap[sn], sv)
        self.q[engname].append(emit)

    def flush(self):
        qs = {e: q for e, q in self.q.items() if q}
        if not qs:
            return
        self.q = {e: [] for e in self.ENGS}
        with self.nc.Block() as blk:
            if "sp" in qs:
                @blk.sync
                def _(e, lst=qs["sp"]):
                    for cl in lst:
                        cl(e)
            if "pe" in qs:
                @blk.tensor
                def _(e, lst=qs["pe"]):
                    for cl in lst:
                        cl(e)
            if "dve" in qs:
                @blk.vector
                def _(e, lst=qs["dve"]):
                    for cl in lst:
                        cl(e)
            if "act" in qs:
                @blk.scalar
                def _(e, lst=qs["act"]):
                    for cl in lst:
                        cl(e)
            if "pl" in qs:
                @blk.gpsimd
                def _(e, lst=qs["pl"]):
                    for cl in lst:
                        cl(e)


class Launch:
    def __init__(self, name):
        self.name = name
        self.nc = bass.Bass("TRN2", target_bir_lowering=False)
        self.es = ExitStack()
        self.ctx = Ctx(self.nc, self.es)

    def run(self, in_maps):
        self.ctx.flush()
        try:
            res = run_bass_kernel_spmd(self.nc, in_maps, core_ids=list(range(NCORE)))
        finally:
            self.es.close()
        return res.results



def TAP(buf, off, dims):
    h = buf.ap_
    if isinstance(h, bass.AP):
        h = h.tensor
    return bass.AP(h, off, dims)

def dma(ctx, dst, src, reads=(), writes=(), eng="sp", transpose=False):
    def fn(e, dst=dst, src=src, transpose=transpose):
        return e.dma_start(out=dst, in_=src, transpose=transpose)
    return ctx.op(eng, fn, reads=reads, writes=writes, dma=True)


class Rot:
    def __init__(self, bufs):
        self.bufs = list(bufs)
        self.i = -1

    def next(self):
        self.i = (self.i + 1) % len(self.bufs)
        return self.bufs[self.i]


def emit_prep(ctx, fused):
    ctx.constcol(0.5); ctx.constcol(1e-5)
    t = ctx.t
    with ExitStack() as pes:
        sb_in = ctx.sbuf("p0_in", [128, 2048], F32, pes)
        sb_pk = ctx.sbuf("p0_pk", [128, 1024], I32, pes)
        sb_sc = ctx.sbuf("p0_sc", [128, 64], F32, pes)
        sb_bf = ctx.sbuf("p0_bf", [128, 2048], BF16, pes)
        sb_red = ctx.sbuf("p0_red", [128, 8], F32, pes)
        sb_gam = ctx.sbuf("p0_gam", [128, 8], F32, pes)
        sb_tau = ctx.sbuf("p0_tau", [128, 8], F32, pes)
        sb_row = ctx.sbuf("p0_row", [1, 8], F32, pes)
        sb_invn = ctx.sbuf("p0_invn", [1, 8], F32, pes)
        sb_iden = ctx.sbuf("p0_iden", [128, 128], BF16, pes)
        sb_ones = ctx.sbuf("p0_ones", [128, 8], F32, pes)
        sb_onesr = ctx.sbuf("p0_onesr", [1, 128], F32, pes)
        sb_tp = Rot([ctx.sbuf(f"p0_tpo{i}", [128, 128], BF16, pes) for i in range(4)])
        si_a = ctx.sbuf("p0_ia", [128, 1024], I32, pes)
        si_b = ctx.sbuf("p0_ib", [128, 1024], I32, pes)
        si_c = ctx.sbuf("p0_ic", [128, 1024], I32, pes)
        sf_j = ctx.sbuf("p0_j", [128, 1024], F32, pes)
        sf_c = ctx.sbuf("p0_c", [128, 1024], F32, pes)
        sf_s = ctx.sbuf("p0_s", [128, 1024], F32, pes)
        sf_m = ctx.sbuf("p0_m", [128, 1024], F32, pes)
        ps_t = Rot([ctx.psum(f"p0_pt{i}", [128, 1024], BF16, pes) for i in range(2)])
        ps_g = ctx.psum("p0_pg", [128, 512], F32, pes)

        dma(ctx, sb_iden[:, :], t["c_iden"][:, :], reads=[t["c_iden"]], writes=[sb_iden])
        dma(ctx, sb_invn[:, :], t["c_invN"][:, :], reads=[t["c_invN"]], writes=[sb_invn])
        ctx.op("dve", lambda e: e.memset(sb_gam.ap_[:, :], 0.0), writes=[sb_gam])
        ctx.op("dve", lambda e: e.memset(sb_ones.ap_[:, :], 1.0), writes=[sb_ones])
        ctx.op("dve", lambda e: e.memset(sb_onesr.ap_[:, :], 1.0), writes=[sb_onesr])

        for mi, wname in enumerate(("w1", "w2", "w3")):
            key, O, F, _k = WSPECS[wname]
            fsl = F // 8
            src = t[f"wsh_{wname}"]
            for oi in range(0, O, 128):
                dma(ctx, sb_in[:, 0:fsl], src[oi:oi + 128, :], reads=[src], writes=[sb_in])
                ctx.op("dve", lambda e, fsl=fsl: e.reduce_sum(
                    sb_red.ap_[:, 0:1], sb_in.ap_[:, 0:fsl], AXX, apply_absolute_value=True),
                    reads=[sb_in], writes=[sb_red])
                ctx.op("dve", lambda e, mi=mi: e.tensor_tensor(
                    sb_gam.ap_[:, mi:mi + 1], sb_gam.ap_[:, mi:mi + 1], sb_red.ap_[:, 0:1], ALU.add),
                    reads=[sb_red, sb_gam], writes=[sb_gam])
        dma(ctx, t["gam_in"].ap_, sb_gam[:, 0:4], reads=[sb_gam], writes=[t["gam_in"]])
        ctx.op("pl", lambda e: e.collective_compute(
            "AllReduce", ALU.add, replica_groups=[list(range(NCORE))],
            ins=[t["gam_in"].ap_.opt()], outs=[t["gam_out"].ap_.opt()]),
            reads=[t["gam_in"]], writes=[t["gam_out"]], sem="cc_gam")
        dma(ctx, sb_gam[:, 0:4], t["gam_out"].ap_, reads=[t["gam_out"]], writes=[sb_gam])
        ctx.op("pe", lambda e: e.matmul(ps_g.ap_[0:1, 0:4], sb_ones.ap_[:, 0:1],
                                        sb_gam.ap_[:, 0:4], start=True, stop=True),
               reads=[sb_ones, sb_gam], writes=[ps_g])
        ctx.op("dve", lambda e: e.tensor_tensor(sb_row.ap_[0:1, 0:4], ps_g.ap_[0:1, 0:4],
                                                sb_invn.ap_[0:1, 0:4], ALU.mult),
               reads=[ps_g, sb_invn], writes=[sb_row])
        ctx.op("dve", lambda e: e.tensor_scalar(sb_row.ap_[0:1, 0:4], sb_row.ap_[0:1, 0:4],
                                                1e-8, 0.5, ALU.add, ALU.mult),
               reads=[sb_row], writes=[sb_row])
        ctx.op("pe", lambda e: e.matmul(ps_g.ap_[:, 4:8], sb_onesr.ap_[0:1, :],
                                        sb_row.ap_[0:1, 0:4], start=True, stop=True),
               reads=[sb_onesr, sb_row], writes=[ps_g])
        ctx.op("dve", lambda e: e.tensor_copy(sb_tau.ap_[:, 0:4], ps_g.ap_[:, 4:8]),
               reads=[ps_g], writes=[sb_tau])
        ctx.flush()

        for wname in WNAMES:
            key, O, F, kind = WSPECS[wname]
            fsl = F // 8
            wt_shard = t[f"wtsh_{wname}"]
            for oi in range(0, O, 128):
                if kind == "plain":
                    src = t[f"wsh_{wname}"]
                    dma(ctx, sb_in[:, 0:fsl], src[oi:oi + 128, :], reads=[src], writes=[sb_in])
                    ctx.op("act", lambda e, fsl=fsl: e.copy(sb_bf.ap_[:, 0:fsl], sb_in.ap_[:, 0:fsl]),
                           reads=[sb_in], writes=[sb_bf])
                elif kind == "bitnet":
                    src = t[f"wsh_{wname}"]
                    mi = BIT_IDX[wname]
                    dma(ctx, sb_in[:, 0:fsl], src[oi:oi + 128, :], reads=[src], writes=[sb_in])
                    ctx.op("dve", lambda e, fsl=fsl, mi=mi: e.tensor_scalar(
                        sf_j.ap_[:, 0:fsl], sb_in.ap_[:, 0:fsl], sb_tau.ap_[:, mi:mi + 1], None, ALU.is_gt),
                        reads=[sb_in, sb_tau], writes=[sf_j])
                    ctx.op("dve", lambda e, fsl=fsl: e.tensor_scalar_mul(
                        sf_c.ap_[:, 0:fsl], sb_in.ap_[:, 0:fsl], -1.0),
                        reads=[sb_in], writes=[sf_c])
                    ctx.op("dve", lambda e, fsl=fsl, mi=mi: e.tensor_scalar(
                        sf_c.ap_[:, 0:fsl], sf_c.ap_[:, 0:fsl], sb_tau.ap_[:, mi:mi + 1], None, ALU.is_gt),
                        reads=[sf_c, sb_tau], writes=[sf_c])
                    ctx.op("dve", lambda e, fsl=fsl: e.tensor_sub(
                        sb_bf.ap_[:, 0:fsl], sf_j.ap_[:, 0:fsl], sf_c.ap_[:, 0:fsl]),
                        reads=[sf_j, sf_c], writes=[sb_bf])
                else:
                    src = t[f"wsh_{wname}"]
                    ssc = t[f"scsh_{wname}"]
                    nb = fsl // 2
                    nblk32 = fsl // 32
                    dma(ctx, sb_pk[:, 0:nb], src[oi:oi + 128, :], reads=[src], writes=[sb_pk])
                    dma(ctx, sb_sc[:, 0:nblk32], ssc[oi:oi + 128, :], reads=[ssc], writes=[sb_sc])
                    for half in range(2):
                        if half == 0:
                            ctx.op("dve", lambda e, nb=nb: e.tensor_single_scalar(
                                si_a.ap_[:, 0:nb], sb_pk.ap_[:, 0:nb], 15, ALU.bitwise_and),
                                reads=[sb_pk], writes=[si_a])
                        else:
                            ctx.op("dve", lambda e, nb=nb: e.tensor_single_scalar(
                                si_a.ap_[:, 0:nb], sb_pk.ap_[:, 0:nb], 4, ALU.arith_shift_right),
                                reads=[sb_pk], writes=[si_a])
                        ctx.op("dve", lambda e, nb=nb: e.tensor_single_scalar(
                            si_b.ap_[:, 0:nb], si_a.ap_[:, 0:nb], 7, ALU.bitwise_and),
                            reads=[si_a], writes=[si_b])
                        ctx.op("dve", lambda e, nb=nb: e.tensor_single_scalar(
                            si_c.ap_[:, 0:nb], si_a.ap_[:, 0:nb], 3, ALU.arith_shift_right),
                            reads=[si_a], writes=[si_c])
                        ctx.op("dve", lambda e, nb=nb: e.tensor_copy(
                            sf_j.ap_[:, 0:nb], si_b.ap_[:, 0:nb]), reads=[si_b], writes=[sf_j])
                        ctx.op("dve", lambda e, nb=nb: e.tensor_copy(
                            sf_s.ap_[:, 0:nb], si_c.ap_[:, 0:nb]), reads=[si_c], writes=[sf_s])
                        ctx.op("dve", lambda e, nb=nb: e.tensor_scalar(
                            si_b.ap_[:, 0:nb], si_b.ap_[:, 0:nb], 4194304, 1056964608, ALU.mult, ALU.add),
                            reads=[si_b], writes=[si_b])
                        cb = ctx.constcol(0.5)
                        ctx.op("act", lambda e, nb=nb, cb=cb: e.activation(
                            sf_c.ap_[:, 0:nb], sf_j.ap_[:, 0:nb], AF.Relu, bias=cb.ap_[:, 0:1],
                            scale=-0.25), reads=[sf_j, cb], writes=[sf_c])
                        ctx.op("dve", lambda e, nb=nb: e.tensor_sub(
                            sf_m.ap_[:, 0:nb], si_b.ap_[:, 0:nb].bitcast(F32), sf_c.ap_[:, 0:nb]),
                            reads=[si_b, sf_c], writes=[sf_m])
                        ctx.op("dve", lambda e, nb=nb: e.tensor_scalar(
                            sf_s.ap_[:, 0:nb], sf_s.ap_[:, 0:nb], -2.0, 1.0, ALU.mult, ALU.add),
                            reads=[sf_s], writes=[sf_s])
                        ctx.op("dve", lambda e, nb=nb: e.tensor_mul(
                            sf_m.ap_[:, 0:nb], sf_m.ap_[:, 0:nb], sf_s.ap_[:, 0:nb]),
                            reads=[sf_m, sf_s], writes=[sf_m])
                        ctx.op("dve", lambda e, nb=nb, nblk32=nblk32, half=half: e.tensor_tensor(
                            bass.AP(sb_bf.ap_, half, [[2048, 128], [32, nblk32], [2, 16]]),
                            bass.AP(sf_m.ap_, 0, [[1024, 128], [16, nblk32], [1, 16]]),
                            bass.AP(sb_sc.ap_, 0, [[64, 128], [1, nblk32], [0, 16]]),
                            ALU.mult),
                            reads=[sf_m, sb_sc], writes=[sb_bf])
                for bi in range((fsl + 127) // 128):
                    bs = bi * 128
                    bn = min(128, fsl - bs)
                    pt = ps_t.next()
                    so = sb_tp.next()
                    ctx.op("pe", lambda e, bs=bs, bn=bn, pt=pt: e.transpose(
                        pt.ap_[0:bn, 0:128], sb_bf.ap_[:, bs:bs + bn], sb_iden.ap_[:, :]),
                        reads=[sb_bf, sb_iden], writes=[pt])
                    ctx.op("dve", lambda e, bn=bn, pt=pt, so=so: e.tensor_copy(
                        so.ap_[0:bn, :], pt.ap_[0:bn, 0:128]), reads=[pt], writes=[so])
                    dma(ctx, wt_shard.ap_[bs:bs + bn, oi:oi + 128],
                        so[0:bn, :], reads=[so], writes=[wt_shard], eng="pl")
        ctx.flush()
        if fused:
            for wname in WNAMES:
                ctx.op("pl", lambda e, wname=wname: e.collective_compute(
                    "AllGather", ALU.bypass, replica_groups=[list(range(NCORE))],
                    ins=[ctx.t[f"wtsh_{wname}"].ap_.opt()], outs=[ctx.t[f"wt_{wname}"].ap_.opt()]),
                    reads=[ctx.t[f"wtsh_{wname}"]], writes=[ctx.t[f"wt_{wname}"]], sem="cc")
        ctx.flush()


def load_bias_mat(ctx, pes, name, bias_t, ps_pool, n=D):
    row = ctx.sbuf(f"{name}_row", [1, n], F32, pes)
    ones = ctx.sbuf(f"{name}_one", [1, 128], F32, pes)
    mat = ctx.sbuf(f"{name}_mat", [128, n], F32, pes)
    dma(ctx, row[0:1, :], bias_t.ap_[0:1, :], reads=[bias_t], writes=[row])
    ctx.op("dve", lambda e: e.memset(ones.ap_[:, :], 1.0), writes=[ones])
    for ci in range(0, n, 512):
        ps = ps_pool.next()
        ctx.op("pe", lambda e, ci=ci, ps=ps: e.matmul(
            ps.ap_[:, 0:512], ones.ap_[0:1, :], row.ap_[0:1, ci:ci + 512],
            start=True, stop=True), reads=[ones, row], writes=[ps])
        ctx.op("dve", lambda e, ci=ci, ps=ps: e.tensor_copy(mat.ap_[:, ci:ci + 512], ps.ap_[:, 0:512]),
               reads=[ps], writes=[mat])
    return mat


def emit_ln_tile(ctx, xsum, scr, st1, st2, out_f32, out_bf, n=D):
    ctx.op("dve", lambda e: e.reduce_sum(st1.ap_[:, 0:1], xsum.ap_[:, 0:n], AXX),
           reads=[xsum], writes=[st1])
    ctx.op("act", lambda e: e.activation(scr.ap_[:, 0:n], xsum.ap_[:, 0:n], AF.Square,
                                         accum_out=st1.ap_[:, 1:2]),
           reads=[xsum], writes=[scr, st1])
    ctx.op("dve", lambda e: e.tensor_scalar_mul(st1.ap_[:, 2:4], st1.ap_[:, 0:2], 1.0 / n),
           reads=[st1], writes=[st1])
    ctx.op("dve", lambda e: e.tensor_tensor(st2.ap_[:, 0:1], st1.ap_[:, 2:3], st1.ap_[:, 2:3], ALU.mult),
           reads=[st1], writes=[st2])
    ctx.op("dve", lambda e: e.tensor_sub(st2.ap_[:, 1:2], st1.ap_[:, 3:4], st2.ap_[:, 0:1]),
           reads=[st1, st2], writes=[st2])
    cb5 = ctx.constcol(1e-5)
    ctx.op("act", lambda e, cb5=cb5: e.activation(st2.ap_[:, 2:3], st2.ap_[:, 1:2], AF.Ln,
                                                  bias=cb5.ap_[:, 0:1]),
           reads=[st2, cb5], writes=[st2])
    ctx.op("act", lambda e: e.activation(st2.ap_[:, 3:4], st2.ap_[:, 2:3], AF.Exp, scale=-0.5),
           reads=[st2], writes=[st2])
    ctx.op("dve", lambda e: e.tensor_scalar(out_f32.ap_[:, 0:n], xsum.ap_[:, 0:n],
                                            st1.ap_[:, 2:3], st2.ap_[:, 3:4],
                                            ALU.subtract, ALU.mult),
           reads=[xsum, st1, st2], writes=[out_f32])
    if out_bf is not None:
        ctx.op("dve", lambda e: e.tensor_copy(out_bf.ap_[:, 0:n], out_f32.ap_[:, 0:n]),
               reads=[out_f32], writes=[out_bf])


def load_xfm(ctx, xbf_t, xfm, row0, ntok):
    for fc in range(16):
        dma(ctx, xfm[:, fc, 0:ntok],
            xbf_t.ap_[row0:row0 + ntok, fc * 128:(fc + 1) * 128],
            reads=[xbf_t], writes=[xfm], transpose=True)


def emit_p1(ctx):
    ctx.constcol(0.5); ctx.constcol(1e-5)
    t = ctx.t
    with ExitStack() as pes:
        xin = Rot([ctx.sbuf(f"p1_xi{i}", [128, 2048], F32, pes) for i in range(2)])
        xbo = Rot([ctx.sbuf(f"p1_xo{i}", [128, 2048], BF16, pes) for i in range(2)])
        for tc in range(16):
            xi = xin.next()
            xo = xbo.next()
            dma(ctx, xi[:, :], t["x"].ap_[tc * 128:(tc + 1) * 128, :], reads=[t["x"]], writes=[xi])
            ctx.op("dve", lambda e, xi=xi, xo=xo: e.tensor_copy(xo.ap_[:, :], xi.ap_[:, :]),
                   reads=[xi], writes=[xo])
            dma(ctx, t["xbf"].ap_[tc * 128:(tc + 1) * 128, :], xo[:, :],
                reads=[xo], writes=[t["xbf"]], eng="pl")
        ctx.flush()

        xfm = ctx.sbuf("p1_xfm", [128, 16, 512], BF16, pes)
        wbuf = ctx.sbuf("p1_wb", [128, 16, 512], BF16, pes)
        wgbuf = ctx.sbuf("p1_wgb", [128, 16, 128], BF16, pes)
        kblk = ctx.sbuf("p1_kblk", [128, 4, 2048], BF16, pes)
        vblk = ctx.sbuf("p1_vblk", [128, 4, 2048], BF16, pes)
        qpiece = Rot([ctx.sbuf(f"p1_qp{i}", [128, 512], F32, pes) for i in range(3)])
        rp1 = ctx.sbuf("p1_rp1", [128, 512], F32, pes)
        rp2 = ctx.sbuf("p1_rp2", [128, 512], F32, pes)
        rp3 = ctx.sbuf("p1_rp3", [128, 512], F32, pes)
        obf = Rot([ctx.sbuf(f"p1_ob{i}", [128, 512], BF16, pes) for i in range(4)])
        sincos = ctx.sbuf("p1_sc", [128, 4, 128], F32, pes)
        onesb = ctx.sbuf("p1_onesb", [128, 1], BF16, pes)
        iden1 = ctx.sbuf("p1_iden", [128, 128], BF16, pes)
        gf32 = ctx.sbuf("p1_gf32", [128, 512], F32, pes)
        bgcols = ctx.sbuf("p1_bg", [128, 16], F32, pes)
        tpo = Rot([ctx.sbuf(f"p1_tp{i}", [128, 128], BF16, pes) for i in range(4)])
        ps = Rot([ctx.psum(f"p1_ps{i}", [128, 512], F32, pes) for i in range(2)])
        psg = Rot([ctx.psum(f"p1_pg{i}", [128, 512], F32, pes) for i in range(2)])
        pst = Rot([ctx.psum(f"p1_pt{i}", [128, 1024], BF16, pes) for i in range(2)])
        pss = ctx.psum("p1_pss", [128, 512], F32, pes)
        psb = Rot([ctx.psum("p1_psb", [128, 512], F32, pes)])

        ctx.op("dve", lambda e: e.memset(onesb.ap_[:, :], 1.0), writes=[onesb])
        dma(ctx, iden1[:, :], t["c_iden"][:, :], reads=[t["c_iden"]], writes=[iden1])
        for gc in range(16):
            dma(ctx, bgcols[:, gc:gc + 1], TAP(t["bg"], gc * 128, [[1, 128], [1, 1]]),
                reads=[t["bg"]], writes=[bgcols])
        bias = {}
        for pname in ("bq", "bk", "bv"):
            bias[pname] = load_bias_mat(ctx, pes, f"p1_{pname}", t[pname], psb)
        ctx.flush()

        for b in range(4):
            row0 = b * 512
            load_xfm(ctx, t["xbf"], xfm, row0, 512)
            for tc4 in range(4):
                dma(ctx, sincos[:, tc4, 0:64],
                    t["c_rsin"].ap_[row0 + tc4 * 128: row0 + (tc4 + 1) * 128, :],
                    reads=[t["c_rsin"]], writes=[sincos])
                dma(ctx, sincos[:, tc4, 64:128],
                    t["c_rcos"].ap_[row0 + tc4 * 128: row0 + (tc4 + 1) * 128, :],
                    reads=[t["c_rcos"]], writes=[sincos])
            for pi_, pname in enumerate(("q0", "k0", "v0")):
                bmat = bias[("bq", "bk", "bv")[pi_]]
                for ot in range(4):
                    dma(ctx, wbuf[:, :, :],
                        TAP(t[f"wt_{pname}"], ot * 512, [[2048, 128], [128 * 2048, 16], [1, 512]]),
                        reads=[t[f"wt_{pname}"]], writes=[wbuf])
                    for tc4 in range(4):
                        p = ps.next()
                        for fc in range(16):
                            ctx.op("pe", lambda e, fc=fc, tc4=tc4, p=p: e.matmul(
                                p.ap_[:, :], xfm.ap_[:, fc, tc4 * 128:(tc4 + 1) * 128],
                                wbuf.ap_[:, fc, :], start=(fc == 0), stop=(fc == 15)),
                                reads=[xfm, wbuf], writes=[p])
                        if pname == "v0":
                            ob = obf.next()
                            ctx.op("dve", lambda e, p=p, ot=ot, ob=ob, bmat=bmat: e.tensor_tensor(
                                ob.ap_[:, :], p.ap_[:, :], bmat.ap_[:, ot * 512:(ot + 1) * 512], ALU.add),
                                reads=[p, bmat], writes=[ob])
                            ctx.op("dve", lambda e, ob=ob, tc4=tc4, ot=ot: e.tensor_copy(
                                vblk.ap_[:, tc4, ot * 512:(ot + 1) * 512], ob.ap_[:, :]),
                                reads=[ob], writes=[vblk])
                            dma(ctx, t["v0tok"].ap_[row0 + tc4 * 128:row0 + (tc4 + 1) * 128,
                                                    ot * 512:(ot + 1) * 512],
                                ob[:, :], reads=[ob], writes=[t["v0tok"]], eng="pl")
                        else:
                            qp = qpiece.next()
                            ctx.op("dve", lambda e, p=p, ot=ot, qp=qp, bmat=bmat: e.tensor_tensor(
                                qp.ap_[:, :], p.ap_[:, :], bmat.ap_[:, ot * 512:(ot + 1) * 512], ALU.add),
                                reads=[p, bmat], writes=[qp])
                            if pname == "q0" and "q0pre" in t:
                                dma(ctx, t["q0pre"].ap_[row0 + tc4 * 128:row0 + (tc4 + 1) * 128,
                                                        ot * 512:(ot + 1) * 512],
                                    qp[:, :], reads=[qp], writes=[t["q0pre"]], eng="pl")
                            qe = bass.AP(qp.ap_, 0, [[512, 128], [128, 4], [2, 64]])
                            qo = bass.AP(qp.ap_, 1, [[512, 128], [128, 4], [2, 64]])
                            sn = bass.AP(sincos.ap_, tc4 * 128, [[512, 128], [0, 4], [1, 64]])
                            cs = bass.AP(sincos.ap_, tc4 * 128 + 64, [[512, 128], [0, 4], [1, 64]])
                            r1e = bass.AP(rp1.ap_, 0, [[512, 128], [128, 4], [2, 64]])
                            r1o = bass.AP(rp1.ap_, 1, [[512, 128], [128, 4], [2, 64]])
                            r2e = bass.AP(rp2.ap_, 0, [[512, 128], [128, 4], [2, 64]])
                            r2o = bass.AP(rp2.ap_, 1, [[512, 128], [128, 4], [2, 64]])
                            ctx.op("dve", lambda e, qe=qe, cs=cs, r1e=r1e: e.tensor_tensor(
                                r1e, qe, cs, ALU.mult), reads=[qp, sincos], writes=[rp1])
                            ctx.op("dve", lambda e, qo=qo, sn=sn, r2e=r2e: e.tensor_tensor(
                                r2e, qo, sn, ALU.mult), reads=[qp, sincos], writes=[rp2])
                            ctx.op("dve", lambda e, qe=qe, sn=sn, r1o=r1o: e.tensor_tensor(
                                r1o, qe, sn, ALU.mult), reads=[qp, sincos], writes=[rp1])
                            ctx.op("dve", lambda e, qo=qo, cs=cs, r2o=r2o: e.tensor_tensor(
                                r2o, qo, cs, ALU.mult), reads=[qp, sincos], writes=[rp2])
                            r3e = bass.AP(rp3.ap_, 0, [[512, 128], [128, 4], [2, 64]])
                            r3o = bass.AP(rp3.ap_, 1, [[512, 128], [128, 4], [2, 64]])
                            ctx.op("dve", lambda e, r1e=r1e, r2e=r2e, r3e=r3e: e.tensor_tensor(
                                r3e, r1e, r2e, ALU.subtract), reads=[rp1, rp2], writes=[rp3])
                            ctx.op("dve", lambda e, r1o=r1o, r2o=r2o, r3o=r3o: e.tensor_tensor(
                                r3o, r1o, r2o, ALU.add), reads=[rp1, rp2], writes=[rp3])
                            ctx.op("dve", lambda e: e.tensor_scalar_min(
                                rp1.ap_[:, :], rp3.ap_[:, :], 0.0), reads=[rp3], writes=[rp1])
                            ctx.op("act", lambda e: e.activation(
                                rp2.ap_[:, :], rp1.ap_[:, :], AF.Exp), reads=[rp1], writes=[rp2])
                            ctx.op("dve", lambda e: e.tensor_scalar_max(
                                rp1.ap_[:, :], rp3.ap_[:, :], 0.0), reads=[rp3], writes=[rp1])
                            ob = obf.next()
                            ctx.op("dve", lambda e, ob=ob: e.tensor_add(
                                ob.ap_[:, :], rp1.ap_[:, :], rp2.ap_[:, :]),
                                reads=[rp1, rp2], writes=[ob])
                            if pname == "k0":
                                ctx.op("dve", lambda e, ob=ob, tc4=tc4, ot=ot: e.tensor_copy(
                                    kblk.ap_[:, tc4, ot * 512:(ot + 1) * 512], ob.ap_[:, :]),
                                    reads=[ob], writes=[kblk])
                                dma(ctx, t["k0tok"].ap_[row0 + tc4 * 128:row0 + (tc4 + 1) * 128,
                                                        ot * 512:(ot + 1) * 512],
                                    ob[:, :], reads=[ob], writes=[t["k0tok"]], eng="pl")
                            else:
                                for oti in range(4):
                                    pt = pst.next()
                                    so = tpo.next()
                                    ctx.op("pe", lambda e, ob=ob, oti=oti, pt=pt: e.transpose(
                                        pt.ap_[:, 0:128], ob.ap_[:, oti * 128:(oti + 1) * 128],
                                        iden1.ap_[:, :]), reads=[ob, iden1], writes=[pt])
                                    ctx.op("dve", lambda e, pt=pt, so=so: e.tensor_copy(
                                        so.ap_[:, :], pt.ap_[:, 0:128]), reads=[pt], writes=[so])
                                    dma(ctx, t["q0fm"].ap_[ot * 512 + oti * 128:ot * 512 + (oti + 1) * 128,
                                                           row0 + tc4 * 128:row0 + (tc4 + 1) * 128],
                                        so[:, :], reads=[so], writes=[t["q0fm"]], eng="pl")
            for gc in range(16):
                dma(ctx, wgbuf[:, :, :],
                    TAP(t["wt_g0"], gc * 128, [[2048, 128], [128 * 2048, 16], [1, 128]]),
                    reads=[t["wt_g0"]], writes=[wgbuf])
                pg = psg.next()
                for fc in range(16):
                    ctx.op("pe", lambda e, fc=fc, pg=pg: e.matmul(
                        pg.ap_[:, :], wgbuf.ap_[:, fc, :], xfm.ap_[:, fc, 0:512],
                        start=(fc == 0), stop=(fc == 15)),
                        reads=[wgbuf, xfm], writes=[pg])
                ctx.op("act", lambda e, pg=pg, gc=gc: e.activation(
                    gf32.ap_[:, :], pg.ap_[:, :],
                    AF.Sigmoid, bias=bgcols.ap_[:, gc:gc + 1]),
                    reads=[pg, bgcols], writes=[gf32])
                dma(ctx, t["g0fm"].ap_[gc * 128:(gc + 1) * 128, row0:row0 + 512],
                    gf32[:, :], reads=[gf32], writes=[t["g0fm"]], eng="pl")
            for h in range(H):
                for tc4 in range(4):
                    ctx.op("pe", lambda e, h=h, tc4=tc4: e.matmul(
                        pss.ap_[:, 0:128], kblk.ap_[:, tc4, h * 128:(h + 1) * 128],
                        vblk.ap_[:, tc4, h * 128:(h + 1) * 128],
                        start=(tc4 == 0), stop=(tc4 == 3)),
                        reads=[kblk, vblk], writes=[pss])
                for tc4 in range(4):
                    ctx.op("pe", lambda e, h=h, tc4=tc4: e.matmul(
                        pss.ap_[:, 128:129], kblk.ap_[:, tc4, h * 128:(h + 1) * 128],
                        onesb.ap_[:, 0:1], start=(tc4 == 0), stop=(tc4 == 3)),
                        reads=[kblk, onesb], writes=[pss])
                stout = qpiece.next()
                ctx.op("dve", lambda e, stout=stout: e.tensor_copy(
                    stout.ap_[:, 0:129], pss.ap_[:, 0:129]), reads=[pss], writes=[stout])
                dma(ctx, TAP(t["stot"], b * (H * 128 * 129) + h * (128 * 129),
                                 [[129, 128], [1, 129]]),
                    stout[:, 0:129], reads=[stout], writes=[t["stot"]], eng="pl")
        ctx.flush()


def emit_p3(ctx):
    ctx.constcol(0.5); ctx.constcol(1e-5)
    t = ctx.t
    with ExitStack() as pes:
        S_all = ctx.sbuf("p3_S", [128, 16, 129], F32, pes)
        S_bf = ctx.sbuf("p3_Sbf", [128, 16, 129], BF16, pes)
        S_st = ctx.sbuf("p3_Sst", [128, 16, 129], F32, pes)
        flags = ctx.sbuf("p3_fl", [128, 36], F32, pes)
        qfm = ctx.sbuf("p3_qfm", [128, 16, 512], BF16, pes)
        gfm = ctx.sbuf("p3_gfm", [128, 16, 512], F32, pes)
        kblk = ctx.sbuf("p3_kblk", [128, 4, 2048], BF16, pes)
        vblk = ctx.sbuf("p3_vblk", [128, 4, 2048], BF16, pes)
        aout = ctx.sbuf("p3_ao", [128, 16, 512], BF16, pes)
        rrow = ctx.sbuf("p3_rr", [1, 1024], F32, pes)
        rmat = Rot([ctx.sbuf(f"p3_rm{i}", [128, 256], F32, pes) for i in range(2)])
        o1 = Rot([ctx.sbuf(f"p3_o1{i}", [128, 256], F32, pes) for i in range(2)])
        onesb = ctx.sbuf("p3_onesb", [128, 1], BF16, pes)
        onesf = ctx.sbuf("p3_onesf", [1, 128], F32, pes)
        wobuf = ctx.sbuf("p3_wo", [128, 16, 512], BF16, pes)
        xprev = ctx.sbuf("p3_xp", [128, 2048], F32, pes)
        xsum = ctx.sbuf("p3_xs", [128, 2048], F32, pes)
        xout = ctx.sbuf("p3_xo", [128, 2048], F32, pes)
        xobf = ctx.sbuf("p3_xob", [128, 2048], BF16, pes)
        scr = ctx.sbuf("p3_scr", [128, 2048], BF16, pes)
        st1 = ctx.sbuf("p3_st1", [128, 8], F32, pes)
        st2 = ctx.sbuf("p3_st2", [128, 8], F32, pes)
        ps_kv = ctx.psum("p3_kv", [128, 1024], F32, pes)
        ps_num = Rot([ctx.psum(f"p3_nm{i}", [128, 512], F32, pes) for i in range(2)])
        ps_den = ctx.psum("p3_dn", [128, 1024], F32, pes)
        ps_r = Rot([ctx.psum(f"p3_r{i}", [128, 512], F32, pes) for i in range(2)])
        ps_o = ps_r

        ctx.op("dve", lambda e: e.memset(onesb.ap_[:, :], 1.0), writes=[onesb])
        ctx.op("dve", lambda e: e.memset(onesf.ap_[:, :], 1.0), writes=[onesf])
        dma(ctx, flags[:, :], t["c_flags0"][:, :], reads=[t["c_flags0"]], writes=[flags])
        bo_mat = load_bias_mat(ctx, pes, "p3_bo", t["bo0"], ps_r)
        ctx.flush()

        for s in range(4):
            row0 = s * 512
            for fc in range(16):
                dma(ctx, qfm[:, fc, :], t["q0fm"].ap_[fc * 128:(fc + 1) * 128, row0:row0 + 512],
                    reads=[t["q0fm"]], writes=[qfm])
                dma(ctx, gfm[:, fc, :], t["g0fm"].ap_[fc * 128:(fc + 1) * 128, row0:row0 + 512],
                    reads=[t["g0fm"]], writes=[gfm])
            for tc4 in range(4):
                dma(ctx, kblk[:, tc4, :], t["k0tok"].ap_[row0 + tc4 * 128:row0 + (tc4 + 1) * 128, :],
                    reads=[t["k0tok"]], writes=[kblk])
                dma(ctx, vblk[:, tc4, :], t["v0tok"].ap_[row0 + tc4 * 128:row0 + (tc4 + 1) * 128, :],
                    reads=[t["v0tok"]], writes=[vblk])
            ctx.op("dve", lambda e: e.memset(S_all.ap_[:, :, :], 0.0), writes=[S_all])
            fbase = sum(PROV[:s])
            for tt in range(PROV[s]):
                r = stot_row(tt)
                dma(ctx, S_st[:, :, :],
                    TAP(t["stot_all"], r * (H * 128 * 129),
                            [[129, 128], [128 * 129, 16], [1, 129]]),
                    reads=[t["stot_all"]], writes=[S_st])
                ctx.op("dve", lambda e, tt=tt, fbase=fbase: e.scalar_tensor_tensor(
                    S_all.ap_[:, :, :], S_st.ap_[:, :, :], flags.ap_[:, fbase + tt:fbase + tt + 1],
                    S_all.ap_[:, :, :], ALU.mult, ALU.add),
                    reads=[S_st, flags, S_all], writes=[S_all])
            for c in range(8):
                tc4, half = divmod(c, 2)
                pbase = half * 64
                for w in range(4):
                    pkv = ps_kv
                    for hh in range(4):
                        h = w * 4 + hh
                        ctx.op("pe", lambda e, h=h, hh=hh, tc4=tc4, pbase=pbase, pkv=pkv: e.matmul(
                            pkv.ap_[:, hh * 256:hh * 256 + 128],
                            kblk.ap_[pbase:pbase + 64, tc4, h * 128:(h + 1) * 128],
                            vblk.ap_[pbase:pbase + 64, tc4, h * 128:(h + 1) * 128],
                            start=True, stop=True),
                            reads=[kblk, vblk], writes=[pkv])
                        ctx.op("pe", lambda e, h=h, hh=hh, tc4=tc4, pbase=pbase, pkv=pkv: e.matmul(
                            pkv.ap_[:, hh * 256 + 128:hh * 256 + 129],
                            kblk.ap_[pbase:pbase + 64, tc4, h * 128:(h + 1) * 128],
                            onesb.ap_[pbase:pbase + 64, 0:1],
                            start=True, stop=True),
                            reads=[kblk, onesb], writes=[pkv])
                    ctx.op("dve", lambda e, w=w, pkv=pkv: e.tensor_tensor(
                        bass.AP(S_all.ap_, w * 4 * 129, [[16 * 129, 128], [129, 4], [1, 129]]),
                        bass.AP(S_all.ap_, w * 4 * 129, [[16 * 129, 128], [129, 4], [1, 129]]),
                        bass.AP(pkv.ap_, 0, [[1024, 128], [256, 4], [1, 129]]),
                        ALU.add), reads=[pkv, S_all], writes=[S_all])
                    ctx.op("dve", lambda e, w=w: e.tensor_copy(
                        bass.AP(S_bf.ap_, w * 4 * 129, [[16 * 129, 128], [129, 4], [1, 129]]),
                        bass.AP(S_all.ap_, w * 4 * 129, [[16 * 129, 128], [129, 4], [1, 129]])),
                        reads=[S_all], writes=[S_bf])
                    pnum = ps_num.bufs[w // 2]
                    for hh in range(4):
                        h = w * 4 + hh
                        col = (w % 2) * 256 + hh * 64
                        ctx.op("pe", lambda e, h=h, col=col, c=c, pnum=pnum: e.matmul(
                            pnum.ap_[:, col:col + 64], S_bf.ap_[:, h, 0:128],
                            qfm.ap_[:, h, c * 64:(c + 1) * 64], start=True, stop=True),
                            reads=[S_bf, qfm], writes=[pnum])
                        ctx.op("pe", lambda e, h=h, c=c: e.matmul(
                            ps_den.ap_[0:1, h * 64:(h + 1) * 64], S_bf.ap_[:, h, 128:129],
                            qfm.ap_[:, h, c * 64:(c + 1) * 64], start=True, stop=True),
                            reads=[S_bf, qfm], writes=[ps_den])
                ctx.op("dve", lambda e: e.tensor_scalar_add(rrow.ap_[0:1, :], ps_den.ap_[0:1, :], 1e-6),
                       reads=[ps_den], writes=[rrow])
                ctx.op("dve", lambda e: e.reciprocal(rrow.ap_[0:1, :], rrow.ap_[0:1, :]),
                       reads=[rrow], writes=[rrow])
                for w2 in range(4):
                    pr = ps_r.next()
                    ctx.op("pe", lambda e, w2=w2, pr=pr: e.matmul(
                        pr.ap_[:, 0:256], onesf.ap_[0:1, :], rrow.ap_[0:1, w2 * 256:(w2 + 1) * 256],
                        start=True, stop=True), reads=[onesf, rrow], writes=[pr])
                    rm = rmat.next()
                    ctx.op("dve", lambda e, pr=pr, rm=rm: e.tensor_copy(rm.ap_[:, :], pr.ap_[:, 0:256]),
                           reads=[pr], writes=[rm])
                    pnum = ps_num.bufs[w2 // 2]
                    oo = o1.next()
                    ctx.op("dve", lambda e, w2=w2, rm=rm, oo=oo, pnum=pnum: e.tensor_tensor(
                        oo.ap_[:, :], pnum.ap_[:, (w2 % 2) * 256:(w2 % 2) * 256 + 256],
                        rm.ap_[:, :], ALU.mult),
                        reads=[pnum, rm], writes=[oo])
                    ctx.op("dve", lambda e, w2=w2, c=c, oo=oo: e.tensor_tensor(
                        bass.AP(aout.ap_, (w2 * 4) * 512 + c * 64, [[16 * 512, 128], [512, 4], [1, 64]]),
                        oo.ap_[:, :].rearrange("p (h t) -> p h t", h=4) if False else
                        bass.AP(oo.ap_, 0, [[256, 128], [64, 4], [1, 64]]),
                        bass.AP(gfm.ap_, (w2 * 4) * 512 + c * 64, [[16 * 512, 128], [512, 4], [1, 64]]),
                        ALU.mult),
                        reads=[oo, gfm], writes=[aout])
            for tc4 in range(4):
                trow = row0 + tc4 * 128
                dma(ctx, xprev[:, :], t["x"].ap_[trow:trow + 128, :], reads=[t["x"]], writes=[xprev])
                ctx.op("dve", lambda e: e.tensor_add(xprev.ap_[:, :], xprev.ap_[:, :], bo_mat.ap_[:, :]),
                       reads=[bo_mat, xprev], writes=[xprev])
                for ot in range(4):
                    dma(ctx, wobuf[:, :, :],
                        TAP(t["wt_o0"], ot * 512, [[2048, 128], [128 * 2048, 16], [1, 512]]),
                        reads=[t["wt_o0"]], writes=[wobuf])
                    po = ps_o.next()
                    for fc in range(16):
                        ctx.op("pe", lambda e, fc=fc, tc4=tc4, po=po: e.matmul(
                            po.ap_[:, :], aout.ap_[:, fc, tc4 * 128:(tc4 + 1) * 128],
                            wobuf.ap_[:, fc, :], start=(fc == 0), stop=(fc == 15)),
                            reads=[aout, wobuf], writes=[po])
                    ctx.op("dve", lambda e, ot=ot, po=po: e.tensor_tensor(
                        xsum.ap_[:, ot * 512:(ot + 1) * 512], po.ap_[:, :],
                        xprev.ap_[:, ot * 512:(ot + 1) * 512], ALU.add),
                        reads=[po, xprev], writes=[xsum])
                emit_ln_tile(ctx, xsum, scr, st1, st2, xout, xobf)
                dma(ctx, t["x1"].ap_[trow:trow + 128, :], xout[:, :], reads=[xout],
                    writes=[t["x1"]], eng="pl")
                dma(ctx, t["x1bf"].ap_[trow:trow + 128, :], xobf[:, :], reads=[xobf],
                    writes=[t["x1bf"]], eng="pl")
        ctx.flush()


def emit_ffn(ctx, pfx, xin_f32, xin_bf, xout_f32, xout_bf, wg, wu, wd, FH, act_kind):
    ctx.constcol(0.5); ctx.constcol(1e-5)
    """x [2048,2048]; hidden FH; act 'sig' (sigmoid(x1)*x2) or 'gelu' (gelu(g)*u)."""
    t = ctx.t
    HC = FH // 128
    halves = [range(0, HC)] if HC <= 64 else [range(0, 64), range(64, HC)]
    HMAX = max(len(hv) for hv in halves)
    af = AF.Sigmoid if act_kind == "sig" else AF.Gelu_apprx_tanh
    with ExitStack() as pes:
        xfm = ctx.sbuf(f"{pfx}_xfm", [128, 16, 512], BF16, pes)
        hid = ctx.sbuf(f"{pfx}_hid", [128, HMAX, 512], BF16, pes)
        wgb = Rot([ctx.sbuf(f"{pfx}_wg{i}", [128, 16, 128], BF16, pes) for i in range(3)])
        wub = Rot([ctx.sbuf(f"{pfx}_wu{i}", [128, 16, 128], BF16, pes) for i in range(3)])
        wdb = Rot([ctx.sbuf(f"{pfx}_wd{i}", [128, 1024], BF16, pes) for i in range(4)])
        gl = ctx.sbuf(f"{pfx}_gl", [128, 512], BF16, pes)
        xacc = ctx.sbuf(f"{pfx}_xa", [128, 4, 2048], F32, pes)
        xout = ctx.sbuf(f"{pfx}_xo", [128, 2048], F32, pes)
        xobf = ctx.sbuf(f"{pfx}_xob", [128, 2048], BF16, pes)
        scr = ctx.sbuf(f"{pfx}_scr", [128, 2048], BF16, pes)
        st1 = ctx.sbuf(f"{pfx}_st1", [128, 8], F32, pes)
        st2 = ctx.sbuf(f"{pfx}_st2", [128, 8], F32, pes)
        psg = Rot([ctx.psum(f"{pfx}_pg{i}", [128, 512], F32, pes) for i in range(2)])
        psu = Rot([ctx.psum(f"{pfx}_pu{i}", [128, 512], F32, pes) for i in range(2)])
        psd = [ctx.psum(f"{pfx}_pd{i}", [128, 512], F32, pes) for i in range(4)]

        for tq in range(4):
            trow = tq * 512
            load_xfm(ctx, xin_bf, xfm, trow, 512)
            for tc4 in range(4):
                dma(ctx, xacc[:, tc4, :], xin_f32.ap_[trow + tc4 * 128:trow + (tc4 + 1) * 128, :],
                    reads=[xin_f32], writes=[xacc])
            for hv in halves:
                nh = len(hv)
                for hi, hc in enumerate(hv):
                    wgt = wgb.next()
                    wut = wub.next()
                    dma(ctx, wgt[:, :, :],
                        TAP(wg, hc * 128, [[FH, 128], [128 * FH, 16], [1, 128]]),
                        reads=[wg], writes=[wgt])
                    dma(ctx, wut[:, :, :],
                        TAP(wu, hc * 128, [[FH, 128], [128 * FH, 16], [1, 128]]),
                        reads=[wu], writes=[wut])
                    pg = psg.next()
                    pu = psu.next()
                    for fc in range(16):
                        ctx.op("pe", lambda e, fc=fc, pg=pg, wgt=wgt: e.matmul(
                            pg.ap_[:, :], wgt.ap_[:, fc, :], xfm.ap_[:, fc, :],
                            start=(fc == 0), stop=(fc == 15)), reads=[wgt, xfm], writes=[pg])
                    for fc in range(16):
                        ctx.op("pe", lambda e, fc=fc, pu=pu, wut=wut: e.matmul(
                            pu.ap_[:, :], wut.ap_[:, fc, :], xfm.ap_[:, fc, :],
                            start=(fc == 0), stop=(fc == 15)), reads=[wut, xfm], writes=[pu])
                    ctx.op("act", lambda e, pg=pg: e.activation(gl.ap_[:, :], pg.ap_[:, :], af),
                           reads=[pg], writes=[gl])
                    ctx.op("dve", lambda e, hi=hi, pu=pu: e.tensor_tensor(
                        hid.ap_[:, hi, :], gl.ap_[:, :], pu.ap_[:, :], ALU.mult),
                        reads=[gl, pu], writes=[hid])
                extra = [psg.bufs[0], psg.bufs[1], psu.bufs[0], psu.bufs[1]]
                for opass in range(2):
                    tiles = [(tc4, opass * 2 + oi) for tc4 in range(4) for oi in range(2)]
                    pmap = {}
                    for idx, (tc4, ot) in enumerate(tiles):
                        pmap[(tc4, ot)] = (psd + extra)[idx]
                    for hi, hc in enumerate(hv):
                        wdt = wdb.next()
                        dma(ctx, wdt[:, :],
                            wd.ap_[hc * 128:(hc + 1) * 128, opass * 1024:(opass + 1) * 1024],
                            reads=[wd], writes=[wdt])
                        for oi in range(2):
                            ot = opass * 2 + oi
                            for tc4 in range(4):
                                p = pmap[(tc4, ot)]
                                ctx.op("pe", lambda e, hi=hi, tc4=tc4, oi=oi, p=p, wdt=wdt, nh=nh: e.matmul(
                                    p.ap_[:, :], hid.ap_[:, hi, tc4 * 128:(tc4 + 1) * 128],
                                    wdt.ap_[:, oi * 512:(oi + 1) * 512],
                                    start=(hi == 0), stop=(hi == nh - 1)),
                                    reads=[hid, wdt], writes=[p])
                    for (tc4, ot) in tiles:
                        p = pmap[(tc4, ot)]
                        ctx.op("dve", lambda e, tc4=tc4, ot=ot, p=p: e.tensor_tensor(
                            xacc.ap_[:, tc4, ot * 512:(ot + 1) * 512], p.ap_[:, :],
                            xacc.ap_[:, tc4, ot * 512:(ot + 1) * 512], ALU.add),
                            reads=[p, xacc], writes=[xacc])
            for tc4 in range(4):
                ctx.op("dve", lambda e, tc4=tc4: e.reduce_sum(
                    st1.ap_[:, 0:1], xacc.ap_[:, tc4, :], AXX), reads=[xacc], writes=[st1])
                ctx.op("act", lambda e, tc4=tc4: e.activation(
                    scr.ap_[:, :], xacc.ap_[:, tc4, :], AF.Square, accum_out=st1.ap_[:, 1:2]),
                    reads=[xacc], writes=[scr, st1])
                ctx.op("dve", lambda e: e.tensor_scalar_mul(st1.ap_[:, 2:4], st1.ap_[:, 0:2], 1.0 / D),
                       reads=[st1], writes=[st1])
                ctx.op("dve", lambda e: e.tensor_tensor(st2.ap_[:, 0:1], st1.ap_[:, 2:3],
                                                        st1.ap_[:, 2:3], ALU.mult),
                       reads=[st1], writes=[st2])
                ctx.op("dve", lambda e: e.tensor_sub(st2.ap_[:, 1:2], st1.ap_[:, 3:4], st2.ap_[:, 0:1]),
                       reads=[st1, st2], writes=[st2])
                cb5 = ctx.constcol(1e-5)
                ctx.op("act", lambda e, cb5=cb5: e.activation(st2.ap_[:, 2:3], st2.ap_[:, 1:2],
                                                              AF.Ln, bias=cb5.ap_[:, 0:1]),
                       reads=[st2, cb5], writes=[st2])
                ctx.op("act", lambda e: e.activation(st2.ap_[:, 3:4], st2.ap_[:, 2:3], AF.Exp, scale=-0.5),
                       reads=[st2], writes=[st2])
                ctx.op("dve", lambda e, tc4=tc4: e.tensor_scalar(
                    xout.ap_[:, :], xacc.ap_[:, tc4, :], st1.ap_[:, 2:3], st2.ap_[:, 3:4],
                    ALU.subtract, ALU.mult), reads=[xacc, st1, st2], writes=[xout])
                dma(ctx, xout_f32.ap_[trow + tc4 * 128:trow + (tc4 + 1) * 128, :], xout[:, :],
                    reads=[xout], writes=[xout_f32], eng="pl")
                if xout_bf is not None:
                    ctx.op("dve", lambda e: e.tensor_copy(xobf.ap_[:, :], xout.ap_[:, :]),
                           reads=[xout], writes=[xobf])
                    dma(ctx, xout_bf.ap_[trow + tc4 * 128:trow + (tc4 + 1) * 128, :], xobf[:, :],
                        reads=[xobf], writes=[xout_bf], eng="pl")
        ctx.flush()


def emit_p5(ctx):
    ctx.constcol(0.5); ctx.constcol(1e-5)
    t = ctx.t
    with ExitStack() as pes:
        xfm = ctx.sbuf("p5_xfm", [128, 16, 2048], BF16, pes)
        wbuf = Rot([ctx.sbuf(f"p5_wb{i}", [128, 16, 512], BF16, pes) for i in range(2)])
        qpiece = Rot([ctx.sbuf(f"p5_qp{i}", [128, 512], F32, pes) for i in range(3)])
        rp1 = ctx.sbuf("p5_rp1", [128, 512], F32, pes)
        rp2 = ctx.sbuf("p5_rp2", [128, 512], F32, pes)
        obf = Rot([ctx.sbuf(f"p5_ob{i}", [128, 512], BF16, pes) for i in range(4)])
        thm = ctx.sbuf("p5_thm", [128, 64], F32, pes)
        throw = ctx.sbuf("p5_thr", [1, 64], F32, pes)
        onesf = ctx.sbuf("p5_onef", [1, 128], F32, pes)
        postile = ctx.sbuf("p5_pos", [128, 1], F32, pes)
        freq = ctx.sbuf("p5_fq", [128, 64], F32, pes)
        freq2 = ctx.sbuf("p5_fq2", [128, 64], F32, pes)
        kint = ctx.sbuf("p5_ki", [128, 64], I32, pes)
        kf = ctx.sbuf("p5_kf", [128, 64], F32, pes)
        sincos = ctx.sbuf("p5_sc", [128, 16, 128], F32, pes)
        ps = Rot([ctx.psum(f"p5_ps{i}", [128, 512], F32, pes) for i in range(2)])
        psb = Rot([ctx.psum("p5_psb", [128, 512], F32, pes)])

        ctx.op("dve", lambda e: e.memset(onesf.ap_[:, :], 1.0), writes=[onesf])
        dma(ctx, throw[0:1, :], t["theta"].ap_[0:1, :], reads=[t["theta"]], writes=[throw])
        pb = psb.next()
        ctx.op("pe", lambda e, pb=pb: e.matmul(pb.ap_[:, 0:64], onesf.ap_[0:1, :],
                                               throw.ap_[0:1, :], start=True, stop=True),
               reads=[onesf, throw], writes=[pb])
        ctx.op("dve", lambda e, pb=pb: e.tensor_copy(thm.ap_[:, :], pb.ap_[:, 0:64]),
               reads=[pb], writes=[thm])
        for tc in range(16):
            trow = tc * 128
            dma(ctx, postile[:, :], t["c_pos1"].ap_[trow:trow + 128, :],
                reads=[t["c_pos1"]], writes=[postile])
            ctx.op("dve", lambda e: e.tensor_scalar(freq.ap_[:, :], thm.ap_[:, :],
                                                    postile.ap_[:, 0:1], None, ALU.mult),
                   reads=[thm, postile], writes=[freq])
            ctx.op("dve", lambda e: e.tensor_scalar_add(freq2.ap_[:, :], freq.ap_[:, :], PI / 2),
                   reads=[freq], writes=[freq2])
            for half, fq in ((0, freq), (1, freq2)):
                ctx.op("dve", lambda e, fq=fq: e.tensor_scalar_mul(
                    kf.ap_[:, :], fq.ap_[:, :], 1.0 / (2 * PI)),
                    reads=[fq], writes=[kf])
                ctx.op("dve", lambda e: e.tensor_copy(kint.ap_[:, :], kf.ap_[:, :]),
                       reads=[kf], writes=[kint])
                ctx.op("dve", lambda e: e.tensor_copy(kf.ap_[:, :], kint.ap_[:, :]),
                       reads=[kint], writes=[kf])
                ctx.op("dve", lambda e, fq=fq: e.scalar_tensor_tensor(
                    fq.ap_[:, :], kf.ap_[:, :], -C1, fq.ap_[:, :], ALU.mult, ALU.add),
                    reads=[kf, fq], writes=[fq])
                ctx.op("dve", lambda e, fq=fq: e.scalar_tensor_tensor(
                    fq.ap_[:, :], kf.ap_[:, :], -C2, fq.ap_[:, :], ALU.mult, ALU.add),
                    reads=[kf, fq], writes=[fq])
                ctx.op("dve", lambda e, fq=fq: e.tensor_scalar(
                    fq.ap_[:, :], fq.ap_[:, :], PI, -PI, ALU.min, ALU.max),
                    reads=[fq], writes=[fq])
                ctx.op("act", lambda e, fq=fq, half=half, tc=tc: e.activation(
                    sincos.ap_[:, tc, half * 64:half * 64 + 64], fq.ap_[:, :], AF.Sin),
                    reads=[fq], writes=[sincos])
        for tq in range(4):
            load_xfm(ctx, t["x2bf"], bass_view_xfm(xfm, tq) if False else xfm, tq * 512, 512) \
                if False else None
        for fc in range(16):
            dma(ctx, xfm[:, fc, 0:2048],
                t["x2bf"].ap_[0:2048, fc * 128:(fc + 1) * 128],
                reads=[t["x2bf"]], writes=[xfm], transpose=True)
        for pname, outt in (("q1", "q1tok"), ("k1", "k1tok"), ("v1", "v1tok")):
            for ot in range(4):
                wb = wbuf.next()
                dma(ctx, wb[:, :, :],
                    TAP(t[f"wt_{pname}"], ot * 512,
                        [[2048, 128], [128 * 2048, 16], [1, 512]]),
                    reads=[t[f"wt_{pname}"]], writes=[wb])
                for tc in range(16):
                    trow = tc * 128
                    p = ps.next()
                    for fc in range(16):
                        ctx.op("pe", lambda e, fc=fc, tc=tc, p=p, wb=wb: e.matmul(
                            p.ap_[:, :], xfm.ap_[:, fc, tc * 128:(tc + 1) * 128],
                            wb.ap_[:, fc, :], start=(fc == 0), stop=(fc == 15)),
                            reads=[xfm, wb], writes=[p])
                    ob = obf.next()
                    if pname == "v1":
                        ctx.op("dve", lambda e, p=p, ob=ob: e.tensor_copy(ob.ap_[:, :], p.ap_[:, :]),
                               reads=[p], writes=[ob])
                    else:
                        qp = qpiece.next()
                        ctx.op("dve", lambda e, p=p, qp=qp: e.tensor_copy(qp.ap_[:, :], p.ap_[:, :]),
                               reads=[p], writes=[qp])
                        qe = bass.AP(qp.ap_, 0, [[512, 128], [128, 4], [2, 64]])
                        qo = bass.AP(qp.ap_, 1, [[512, 128], [128, 4], [2, 64]])
                        sn = bass.AP(sincos.ap_, tc * 128, [[2048, 128], [0, 4], [1, 64]])
                        cs = bass.AP(sincos.ap_, tc * 128 + 64, [[2048, 128], [0, 4], [1, 64]])
                        h1 = bass.AP(rp1.ap_, 0, [[512, 128], [128, 4], [1, 64]])
                        h2 = bass.AP(rp1.ap_, 64, [[512, 128], [128, 4], [1, 64]])
                        t1 = bass.AP(rp2.ap_, 0, [[512, 128], [128, 4], [1, 64]])
                        t2 = bass.AP(rp2.ap_, 64, [[512, 128], [128, 4], [1, 64]])
                        ctx.op("dve", lambda e, qe=qe, cs=cs, h1=h1: e.tensor_tensor(
                            h1, qe, cs, ALU.mult), reads=[qp, sincos], writes=[rp1])
                        ctx.op("dve", lambda e, qo=qo, sn=sn, t1=t1: e.tensor_tensor(
                            t1, qo, sn, ALU.mult), reads=[qp, sincos], writes=[rp2])
                        ctx.op("dve", lambda e, qe=qe, sn=sn, h2=h2: e.tensor_tensor(
                            h2, qe, sn, ALU.mult), reads=[qp, sincos], writes=[rp1])
                        ctx.op("dve", lambda e, qo=qo, cs=cs, t2=t2: e.tensor_tensor(
                            t2, qo, cs, ALU.mult), reads=[qp, sincos], writes=[rp2])
                        o_lo = bass.AP(ob.ap_, 0, [[512, 128], [128, 4], [1, 64]])
                        o_hi = bass.AP(ob.ap_, 64, [[512, 128], [128, 4], [1, 64]])
                        ctx.op("dve", lambda e, h1=h1, t1=t1, o_lo=o_lo: e.tensor_tensor(
                            o_lo, h1, t1, ALU.subtract), reads=[rp1, rp2], writes=[ob])
                        ctx.op("dve", lambda e, h2=h2, t2=t2, o_hi=o_hi: e.tensor_tensor(
                            o_hi, h2, t2, ALU.add), reads=[rp1, rp2], writes=[ob])
                    dma(ctx, t[outt].ap_[trow:trow + 128, ot * 512:(ot + 1) * 512], ob[:, :],
                        reads=[ob], writes=[t[outt]], eng="pl")
        ctx.flush()


def emit_p7(ctx):
    ctx.constcol(0.5); ctx.constcol(1e-5)
    t = ctx.t
    PR = 32
    with ExitStack() as pes:
        khead = ctx.sbuf("p7_kh", [128, 16 * 512], BF16, pes)
        vhead = ctx.sbuf("p7_vh", [128, 64, 128], BF16, pes)
        qhead = ctx.sbuf("p7_qh", [128, 2048], BF16, pes)
        kdiag = ctx.sbuf("p7_kd", [128, 512], BF16, pes)
        vdiag = ctx.sbuf("p7_vd", [128, 4, 128], BF16, pes)
        sqb = ctx.sbuf("p7_sq", [128, 2048], BF16, pes)
        Pbuf = ctx.sbuf("p7_P", [128, PR * 512], BF16, pes)
        ctxall = ctx.sbuf("p7_ctx", [128, 16, 2048], BF16, pes)
        flagc = ctx.sbuf("p7_flag", [128, 144], F32, pes)
        biasb = ctx.sbuf("p7_bias", [128, 146], F32, pes)
        tric = ctx.sbuf("p7_tri", [128, 4 * 512], BF16, pes)
        onesb = ctx.sbuf("p7_onesb", [128, 1], BF16, pes)
        onesf = ctx.sbuf("p7_onesf", [1, 128], F32, pes)
        nrm = ctx.sbuf("p7_nrm", [1, 8], F32, pes)
        rrow = ctx.sbuf("p7_rr", [1, 512], F32, pes)
        rmat = ctx.sbuf("p7_rm", [128, 512], F32, pes)
        wobuf = ctx.sbuf("p7_wo", [128, 16, 512], BF16, pes)
        xprev = ctx.sbuf("p7_xp", [128, 2048], F32, pes)
        xout = ctx.sbuf("p7_xo", [128, 2048], F32, pes)
        xobf = ctx.sbuf("p7_xob", [128, 2048], BF16, pes)
        scr = ctx.sbuf("p7_scr", [128, 2048], BF16, pes)
        st1 = ctx.sbuf("p7_st1", [128, 8], F32, pes)
        st2 = ctx.sbuf("p7_st2", [128, 8], F32, pes)
        ps_sc = Rot([ctx.psum(f"p7_sc{i}", [128, 512], F32, pes) for i in range(2)])
        ps_ctx = ctx.psum("p7_ctxp", [128, 512], F32, pes)
        ps_sum = ctx.psum("p7_sump", [128, 512], F32, pes)
        ps_m = Rot([ctx.psum(f"p7_m{i}", [128, 512], F32, pes) for i in range(2)])
        ps_o = Rot([ctx.psum(f"p7_o{i}", [128, 512], F32, pes) for i in range(2)])

        ctx.op("dve", lambda e: e.memset(onesb.ap_[:, :], 1.0), writes=[onesb])
        ctx.op("dve", lambda e: e.memset(onesf.ap_[:, :], 1.0), writes=[onesf])
        dma(ctx, flagc[:, :], t["c_flagcols"][:, :], reads=[t["c_flagcols"]], writes=[flagc])
        dma(ctx, tric[:, :], t["c_tri"][:, :], reads=[t["c_tri"]], writes=[tric])
        bo_mat = load_bias_mat(ctx, pes, "p7_bo", t["bo1"], ps_o)
        ctx.flush()

        for h in range(H):
            for b in range(NBLK):
                p0 = ag_pos(b)
                dma(ctx, khead[:, b * 512:(b + 1) * 512],
                    t["kall"].ap_[p0:p0 + 512, h * 128:(h + 1) * 128],
                    reads=[t["kall"]], writes=[khead], transpose=True)
                dma(ctx, bass.AP(vhead.ap_, b * 4 * 128, [[64 * 128, 128], [128, 4], [1, 128]]),
                    TAP(t["vall"], p0 * 2048 + h * 128,
                        [[2048, 128], [128 * 2048, 4], [1, 128]]),
                    reads=[t["vall"]], writes=[vhead])
            for sq in range(4):
                dma(ctx, qhead[:, sq * 512:(sq + 1) * 512],
                    t["q1tok"].ap_[sq * 512:(sq + 1) * 512, h * 128:(h + 1) * 128],
                    reads=[t["q1tok"]], writes=[qhead], transpose=True)
            ctx.op("dve", lambda e: e.memset(nrm.ap_[:, :], 0.0), writes=[nrm])
            ctx.op("act", lambda e: e.activation(sqb.ap_[:, :], qhead.ap_[:, :], AF.Square),
                   reads=[qhead], writes=[sqb])
            for jj in range(4):
                pm = ps_m.next()
                ctx.op("pe", lambda e, jj=jj, pm=pm: e.matmul(
                    pm.ap_[0:1, 0:512], onesb.ap_[:, 0:1], sqb.ap_[:, jj * 512:(jj + 1) * 512],
                    start=True, stop=True), reads=[onesb, sqb], writes=[pm])
                ctx.op("dve", lambda e, pm=pm: e.reduce_max(
                    nrm.ap_[0:1, 2:3], pm.ap_[0:1, 0:512], AXX), reads=[pm], writes=[nrm])
                ctx.op("dve", lambda e: e.tensor_tensor(
                    nrm.ap_[0:1, 0:1], nrm.ap_[0:1, 0:1], nrm.ap_[0:1, 2:3], ALU.max),
                    reads=[nrm], writes=[nrm])
            for kc in range(4):
                ctx.op("act", lambda e, kc=kc: e.activation(
                    sqb.ap_[:, :], khead.ap_[:, kc * 2048:(kc + 1) * 2048], AF.Square),
                    reads=[khead], writes=[sqb])
                for jj in range(4):
                    pm = ps_m.next()
                    ctx.op("pe", lambda e, jj=jj, pm=pm: e.matmul(
                        pm.ap_[0:1, 0:512], onesb.ap_[:, 0:1], sqb.ap_[:, jj * 512:(jj + 1) * 512],
                        start=True, stop=True), reads=[onesb, sqb], writes=[pm])
                    ctx.op("dve", lambda e, pm=pm: e.reduce_max(
                        nrm.ap_[0:1, 2:3], pm.ap_[0:1, 0:512], AXX), reads=[pm], writes=[nrm])
                    ctx.op("dve", lambda e: e.tensor_tensor(
                        nrm.ap_[0:1, 1:2], nrm.ap_[0:1, 1:2], nrm.ap_[0:1, 2:3], ALU.max),
                        reads=[nrm], writes=[nrm])
            ctx.op("dve", lambda e: e.tensor_tensor(nrm.ap_[0:1, 3:4], nrm.ap_[0:1, 0:1],
                                                    nrm.ap_[0:1, 1:2], ALU.mult),
                   reads=[nrm], writes=[nrm])
            ctx.op("act", lambda e: e.activation(nrm.ap_[0:1, 4:5], nrm.ap_[0:1, 3:4], AF.Ln),
                   reads=[nrm], writes=[nrm])
            ctx.op("act", lambda e: e.activation(nrm.ap_[0:1, 5:6], nrm.ap_[0:1, 4:5], AF.Exp, scale=0.5),
                   reads=[nrm], writes=[nrm])
            ctx.op("dve", lambda e: e.tensor_scalar_mul(nrm.ap_[0:1, 6:7], nrm.ap_[0:1, 5:6],
                                                        -1.0 / 11.313708498984761),
                   reads=[nrm], writes=[nrm])
            pm = ps_m.next()
            ctx.op("pe", lambda e, pm=pm: e.matmul(pm.ap_[:, 0:1], onesf.ap_[0:1, :],
                                                   nrm.ap_[0:1, 6:7], start=True, stop=True),
                   reads=[onesf, nrm], writes=[pm])
            ctx.op("dve", lambda e, pm=pm: e.tensor_copy(biasb.ap_[:, 144:145], pm.ap_[:, 0:1]),
                   reads=[pm], writes=[biasb])
            ctx.op("dve", lambda e: e.tensor_tensor(
                biasb.ap_[:, 0:144], flagc.ap_[:, :],
                bass.AP(biasb.ap_, 144, [[146, 128], [0, 144]]), ALU.add),
                reads=[flagc, biasb], writes=[biasb])

            for s in range(4):
                noff = NOFF[s]
                fbase = sum(NOFF[:s])
                nch = noff + 4
                dma(ctx, kdiag[:, :], t["k1tok"].ap_[s * 512:(s + 1) * 512, h * 128:(h + 1) * 128],
                    reads=[t["k1tok"]], writes=[kdiag], transpose=True)
                dma(ctx, bass.AP(vdiag.ap_, 0, [[4 * 128, 128], [128, 4], [1, 128]]),
                    TAP(t["v1tok"], s * 512 * 2048 + h * 128,
                        [[2048, 128], [128 * 2048, 4], [1, 128]]),
                    reads=[t["v1tok"]], writes=[vdiag])
                for r0 in range(0, nch, PR):
                    r1 = min(r0 + PR, nch)
                    for tt in range(r0, r1):
                        psc = ps_sc.next()
                        pslot = tt - r0
                        if tt < noff:
                            ctx.op("pe", lambda e, tt=tt, s=s, psc=psc: e.matmul(
                                psc.ap_[:, :], khead.ap_[:, tt * 128:(tt + 1) * 128],
                                qhead.ap_[:, s * 512:(s + 1) * 512], start=True, stop=True),
                                reads=[khead, qhead], writes=[psc])
                            ctx.op("act", lambda e, pslot=pslot, psc=psc, fbase=fbase, tt=tt: e.activation(
                                Pbuf.ap_[:, pslot * 512:(pslot + 1) * 512], psc.ap_[:, :], AF.Exp,
                                bias=biasb.ap_[:, fbase + tt:fbase + tt + 1],
                                scale=1.0 / 11.313708498984761),
                                reads=[psc, biasb], writes=[Pbuf])
                        else:
                            i = tt - noff
                            ctx.op("pe", lambda e, i=i, s=s, psc=psc: e.matmul(
                                psc.ap_[:, :], kdiag.ap_[:, i * 128:(i + 1) * 128],
                                qhead.ap_[:, s * 512:(s + 1) * 512], start=True, stop=True),
                                reads=[kdiag, qhead], writes=[psc])
                            ctx.op("act", lambda e, pslot=pslot, psc=psc: e.activation(
                                Pbuf.ap_[:, pslot * 512:(pslot + 1) * 512], psc.ap_[:, :], AF.Exp,
                                bias=biasb.ap_[:, 144:145], scale=1.0 / 11.313708498984761),
                                reads=[psc, biasb], writes=[Pbuf])
                            ctx.op("dve", lambda e, pslot=pslot, i=i: e.tensor_tensor(
                                Pbuf.ap_[:, pslot * 512:(pslot + 1) * 512],
                                Pbuf.ap_[:, pslot * 512:(pslot + 1) * 512],
                                tric.ap_[:, i * 512:(i + 1) * 512], ALU.mult),
                                reads=[tric, Pbuf], writes=[Pbuf])
                    for tt in range(r0, r1):
                        pslot = tt - r0
                        ctx.op("pe", lambda e, pslot=pslot, tt=tt, nch=nch: e.matmul(
                            ps_sum.ap_[0:1, :], onesb.ap_[:, 0:1],
                            Pbuf.ap_[:, pslot * 512:(pslot + 1) * 512],
                            start=(tt == 0), stop=(tt == nch - 1)),
                            reads=[onesb, Pbuf], writes=[ps_sum])
                    for tt in range(r0, r1):
                        pslot = tt - r0
                        if tt < noff:
                            lhs = vhead.ap_[:, tt, :]
                            rd = vhead
                        else:
                            lhs = vdiag.ap_[:, tt - noff, :]
                            rd = vdiag
                        ctx.op("pe", lambda e, pslot=pslot, tt=tt, nch=nch, lhs=lhs: e.matmul(
                            ps_ctx.ap_[:, :], lhs, Pbuf.ap_[:, pslot * 512:(pslot + 1) * 512],
                            start=(tt == 0), stop=(tt == nch - 1)),
                            reads=[rd, Pbuf], writes=[ps_ctx])
                ctx.op("dve", lambda e: e.reciprocal(rrow.ap_[0:1, :], ps_sum.ap_[0:1, :]),
                       reads=[ps_sum], writes=[rrow])
                pm = ps_m.next()
                ctx.op("pe", lambda e, pm=pm: e.matmul(pm.ap_[:, :], onesf.ap_[0:1, :],
                                                       rrow.ap_[0:1, :], start=True, stop=True),
                       reads=[onesf, rrow], writes=[pm])
                ctx.op("dve", lambda e, pm=pm: e.tensor_copy(rmat.ap_[:, :], pm.ap_[:, :]),
                       reads=[pm], writes=[rmat])
                ctx.op("dve", lambda e, h=h, s=s: e.tensor_tensor(
                    ctxall.ap_[:, h, s * 512:(s + 1) * 512], ps_ctx.ap_[:, :], rmat.ap_[:, :], ALU.mult),
                    reads=[ps_ctx, rmat], writes=[ctxall])
        for tc in range(16):
            trow = tc * 128
            dma(ctx, xprev[:, :], t["x2"].ap_[trow:trow + 128, :], reads=[t["x2"]], writes=[xprev])
            ctx.op("dve", lambda e: e.tensor_add(xprev.ap_[:, :], xprev.ap_[:, :], bo_mat.ap_[:, :]),
                   reads=[bo_mat, xprev], writes=[xprev])
            for ot in range(4):
                dma(ctx, wobuf[:, :, :],
                    TAP(t["wt_o1"], ot * 512, [[2048, 128], [128 * 2048, 16], [1, 512]]),
                    reads=[t["wt_o1"]], writes=[wobuf])
                po = ps_o.next()
                for fc in range(16):
                    ctx.op("pe", lambda e, fc=fc, tc=tc, po=po: e.matmul(
                        po.ap_[:, :], ctxall.ap_[:, fc, tc * 128:tc * 128 + 128],
                        wobuf.ap_[:, fc, :], start=(fc == 0), stop=(fc == 15)),
                        reads=[ctxall, wobuf], writes=[po])
                ctx.op("dve", lambda e, ot=ot, po=po: e.tensor_tensor(
                    xprev.ap_[:, ot * 512:(ot + 1) * 512], po.ap_[:, :],
                    xprev.ap_[:, ot * 512:(ot + 1) * 512], ALU.add),
                    reads=[po, xprev], writes=[xprev])
            emit_ln_tile(ctx, xprev, scr, st1, st2, xout, xobf)
            dma(ctx, t["x3"].ap_[trow:trow + 128, :], xout[:, :], reads=[xout],
                writes=[t["x3"]], eng="pl")
            dma(ctx, t["x3bf"].ap_[trow:trow + 128, :], xobf[:, :], reads=[xobf],
                writes=[t["x3bf"]], eng="pl")
        ctx.flush()


def declare_consts(ctx, which):
    cmap = dict(c_iden=([128, 128], BF16), c_invN=([1, 8], F32),
                c_rsin=([2048, 64], F32), c_rcos=([2048, 64], F32),
                c_pos1=([2048, 1], F32), c_flagcols=([128, 144], F32),
                c_flags0=([128, 36], F32), c_tri=([128, 4 * 512], BF16))
    for n in which:
        ctx.dram(n, *cmap[n], kind="ExternalInput")


CONST_SRC = dict(c_iden="iden", c_invN="invN", c_rsin="rope_sin", c_rcos="rope_cos",
                 c_pos1="pos1", c_flagcols="flagcols", c_flags0="flags0", c_tri="tri")


def run_unfused(inputs, upto=99, capture=None):
    metas = [host_meta(c) for c in range(NCORE)]
    consts = [host_consts(m) for m in metas]
    x_full = np.asarray(inputs["L0_attn_x"])
    xs = [np.ascontiguousarray(x_full[metas[c]["g"], metas[c]["tok"], :]) for c in range(NCORE)]
    cap = capture if capture is not None else {}

    L = Launch("prep")
    ctx = L.ctx
    for wname in WNAMES:
        key, O, F, kind = WSPECS[wname]
        fsl = F // 8
        if kind == "nvfp4":
            ctx.dram(f"wsh_{wname}", [O, fsl // 2], I32, kind="ExternalInput")
            ctx.dram(f"scsh_{wname}", [O, fsl // 32], F32, kind="ExternalInput")
        else:
            ctx.dram(f"wsh_{wname}", [O, fsl], F32, kind="ExternalInput")
        ctx.dram(f"wtsh_{wname}", [fsl, O], BF16, kind="ExternalOutput")
    ctx.dram("gam_in", [128, 4], F32, kind="Internal")
    ctx.dram("gam_out", [128, 4], F32, kind="Internal")
    declare_consts(ctx, ["c_iden", "c_invN"])
    emit_prep(ctx, fused=False)
    ctx.final_wait("sp")
    in_maps = []
    for c in range(NCORE):
        m = {}
        for wname in WNAMES:
            key, O, F, kind = WSPECS[wname]
            fsl = F // 8
            if kind == "nvfp4":
                m[f"wsh_{wname}"] = np.ascontiguousarray(np.asarray(inputs[key])[:, c * (fsl // 2):(c + 1) * (fsl // 2)])
                m[f"scsh_{wname}"] = np.ascontiguousarray(np.asarray(inputs[SCKEY[wname]])[:, c * (fsl // 32):(c + 1) * (fsl // 32)])
            else:
                m[f"wsh_{wname}"] = np.ascontiguousarray(np.asarray(inputs[key])[:, c * fsl:(c + 1) * fsl])
        m["c_iden"] = consts[c]["iden"]
        m["c_invN"] = consts[c]["invN"]
        in_maps.append(m)
    r = L.run(in_maps)
    wt = {w: np.concatenate([r[c][f"wtsh_{w}"] for c in range(NCORE)], axis=0) for w in WNAMES}
    cap["wt"] = wt
    if upto <= 1:
        return None

    L = Launch("p1")
    ctx = L.ctx
    ctx.dram("x", [2048, 2048], F32, kind="ExternalInput")
    for w in ("q0", "k0", "v0", "g0"):
        ctx.dram(f"wt_{w}", [D, D], BF16, kind="ExternalInput")
    for bn in ("bq", "bk", "bv", "bg"):
        ctx.dram(bn, [1, D], F32, kind="ExternalInput")
    declare_consts(ctx, ["c_iden", "c_rsin", "c_rcos"])
    ctx.dram("xbf", [2048, 2048], BF16, kind="ExternalOutput")
    ctx.dram("q0fm", [2048, 2048], BF16, kind="ExternalOutput")
    ctx.dram("k0tok", [2048, 2048], BF16, kind="ExternalOutput")
    ctx.dram("v0tok", [2048, 2048], BF16, kind="ExternalOutput")
    ctx.dram("g0fm", [2048, 2048], F32, kind="ExternalOutput")
    ctx.dram("stot", [4, H * 128 * 129], F32, kind="ExternalOutput")
    import os as _os
    if _os.environ.get("P1DEBUG"):
        ctx.dram("q0pre", [2048, 2048], F32, kind="ExternalOutput")
    emit_p1(ctx)
    ctx.final_wait("sp")
    in_maps = []
    for c in range(NCORE):
        m = dict(x=xs[c], wt_q0=wt["q0"], wt_k0=wt["k0"], wt_v0=wt["v0"], wt_g0=wt["g0"],
                 bq=np.asarray(inputs["L0_attn_bq"]).reshape(1, -1),
                 bk=np.asarray(inputs["L0_attn_bk"]).reshape(1, -1),
                 bv=np.asarray(inputs["L0_attn_bv"]).reshape(1, -1),
                 bg=np.asarray(inputs["L0_attn_bg"]).reshape(1, -1),
                 c_iden=consts[c]["iden"], c_rsin=consts[c]["rope_sin"], c_rcos=consts[c]["rope_cos"])
        in_maps.append(m)
    r1 = L.run(in_maps)
    cap["p1"] = r1
    if upto <= 2:
        return None

    stot_all = [np.concatenate([r1[g * 4 + rr]["stot"] for rr in range(4)], axis=0) for g in range(2)]
    L = Launch("p3")
    ctx = L.ctx
    ctx.dram("x", [2048, 2048], F32, kind="ExternalInput")
    ctx.dram("q0fm", [2048, 2048], BF16, kind="ExternalInput")
    ctx.dram("g0fm", [2048, 2048], F32, kind="ExternalInput")
    ctx.dram("k0tok", [2048, 2048], BF16, kind="ExternalInput")
    ctx.dram("v0tok", [2048, 2048], BF16, kind="ExternalInput")
    ctx.dram("stot_all", [16, H * 128 * 129], F32, kind="ExternalInput")
    ctx.dram("wt_o0", [D, D], BF16, kind="ExternalInput")
    ctx.dram("bo0", [1, D], F32, kind="ExternalInput")
    declare_consts(ctx, ["c_flags0"])
    ctx.dram("x1", [2048, 2048], F32, kind="ExternalOutput")
    ctx.dram("x1bf", [2048, 2048], BF16, kind="ExternalOutput")
    emit_p3(ctx)
    ctx.final_wait("sp")
    in_maps = []
    for c in range(NCORE):
        m = dict(x=xs[c], q0fm=r1[c]["q0fm"], g0fm=r1[c]["g0fm"], k0tok=r1[c]["k0tok"],
                 v0tok=r1[c]["v0tok"], stot_all=stot_all[metas[c]["g"]],
                 wt_o0=wt["o0"], bo0=np.asarray(inputs["L0_attn_bo"]).reshape(1, -1),
                 c_flags0=consts[c]["flags0"])
        in_maps.append(m)
    r3 = L.run(in_maps)
    cap["p3"] = r3
    if upto <= 3:
        return None

    L = Launch("p4")
    ctx = L.ctx
    ctx.dram("x1", [2048, 2048], F32, kind="ExternalInput")
    ctx.dram("x1bf", [2048, 2048], BF16, kind="ExternalInput")
    for w in ("w1", "w2", "w3"):
        ctx.dram(f"wt_{w}", [WSPECS[w][2], WSPECS[w][1]], BF16, kind="ExternalInput")
    ctx.dram("x2", [2048, 2048], F32, kind="ExternalOutput")
    ctx.dram("x2bf", [2048, 2048], BF16, kind="ExternalOutput")
    emit_ffn(ctx, "p4", ctx.t["x1"], ctx.t["x1bf"], ctx.t["x2"], ctx.t["x2bf"],
             ctx.t["wt_w1"], ctx.t["wt_w2"], ctx.t["wt_w3"], FH0, "sig")
    ctx.final_wait("sp")
    in_maps = [dict(x1=r3[c]["x1"], x1bf=r3[c]["x1bf"], wt_w1=wt["w1"], wt_w2=wt["w2"],
                    wt_w3=wt["w3"]) for c in range(NCORE)]
    r4 = L.run(in_maps)
    cap["p4"] = r4
    if upto <= 4:
        return None

    L = Launch("p5")
    ctx = L.ctx
    ctx.dram("x2bf", [2048, 2048], BF16, kind="ExternalInput")
    for w in ("q1", "k1", "v1"):
        ctx.dram(f"wt_{w}", [D, D], BF16, kind="ExternalInput")
    ctx.dram("theta", [1, 64], F32, kind="ExternalInput")
    declare_consts(ctx, ["c_pos1"])
    ctx.dram("q1tok", [2048, 2048], BF16, kind="ExternalOutput")
    ctx.dram("k1tok", [2048, 2048], BF16, kind="ExternalOutput")
    ctx.dram("v1tok", [2048, 2048], BF16, kind="ExternalOutput")
    emit_p5(ctx)
    ctx.final_wait("sp")
    in_maps = [dict(x2bf=r4[c]["x2bf"], wt_q1=wt["q1"], wt_k1=wt["k1"], wt_v1=wt["v1"],
                    theta=np.asarray(inputs["L1_attn_theta"]).reshape(1, -1),
                    c_pos1=consts[c]["pos1"]) for c in range(NCORE)]
    r5 = L.run(in_maps)
    cap["p5"] = r5
    if upto <= 5:
        return None

    kall = [np.concatenate([r5[g * 4 + rr]["k1tok"] for rr in range(4)], axis=0) for g in range(2)]
    vall = [np.concatenate([r5[g * 4 + rr]["v1tok"] for rr in range(4)], axis=0) for g in range(2)]
    L = Launch("p7")
    ctx = L.ctx
    ctx.dram("x2", [2048, 2048], F32, kind="ExternalInput")
    ctx.dram("q1tok", [2048, 2048], BF16, kind="ExternalInput")
    ctx.dram("k1tok", [2048, 2048], BF16, kind="ExternalInput")
    ctx.dram("v1tok", [2048, 2048], BF16, kind="ExternalInput")
    ctx.dram("kall", [8192, 2048], BF16, kind="ExternalInput")
    ctx.dram("vall", [8192, 2048], BF16, kind="ExternalInput")
    ctx.dram("wt_o1", [D, D], BF16, kind="ExternalInput")
    ctx.dram("bo1", [1, D], F32, kind="ExternalInput")
    declare_consts(ctx, ["c_flagcols", "c_tri"])
    ctx.dram("x3", [2048, 2048], F32, kind="ExternalOutput")
    ctx.dram("x3bf", [2048, 2048], BF16, kind="ExternalOutput")
    emit_p7(ctx)
    ctx.final_wait("sp")
    in_maps = [dict(x2=r4[c]["x2"], q1tok=r5[c]["q1tok"], k1tok=r5[c]["k1tok"],
                    v1tok=r5[c]["v1tok"], kall=kall[metas[c]["g"]], vall=vall[metas[c]["g"]],
                    wt_o1=wt["o1"], bo1=np.asarray(inputs["L1_attn_bo"]).reshape(1, -1),
                    c_flagcols=consts[c]["flagcols"], c_tri=consts[c]["tri"])
               for c in range(NCORE)]
    r7 = L.run(in_maps)
    cap["p7"] = r7
    if upto <= 6:
        return None

    L = Launch("p8")
    ctx = L.ctx
    ctx.dram("x3", [2048, 2048], F32, kind="ExternalInput")
    ctx.dram("x3bf", [2048, 2048], BF16, kind="ExternalInput")
    for w in ("g1", "u1", "d1"):
        ctx.dram(f"wt_{w}", [WSPECS[w][2], WSPECS[w][1]], BF16, kind="ExternalInput")
    ctx.dram("out", [2048, 2048], F32, kind="ExternalOutput")
    emit_ffn(ctx, "p8", ctx.t["x3"], ctx.t["x3bf"], ctx.t["out"], None,
             ctx.t["wt_g1"], ctx.t["wt_u1"], ctx.t["wt_d1"], FH1, "gelu")
    ctx.final_wait("sp")
    in_maps = [dict(x3=r7[c]["x3"], x3bf=r7[c]["x3bf"], wt_g1=wt["g1"], wt_u1=wt["u1"],
                    wt_d1=wt["d1"]) for c in range(NCORE)]
    r8 = L.run(in_maps)
    cap["p8"] = r8

    out = np.zeros((B, T, D), np.float32)
    for c in range(NCORE):
        out[metas[c]["g"], metas[c]["tok"], :] = r8[c]["out"]
    return out


def run_fused(inputs):
    metas = [host_meta(c) for c in range(NCORE)]
    consts = [host_consts(m) for m in metas]
    x_full = np.asarray(inputs["L0_attn_x"])
    xs = [np.ascontiguousarray(x_full[metas[c]["g"], metas[c]["tok"], :]) for c in range(NCORE)]

    L = Launch("mega")
    ctx = L.ctx
    for wname in WNAMES:
        key, O, F, kind = WSPECS[wname]
        fsl = F // 8
        if kind == "nvfp4":
            ctx.dram(f"wsh_{wname}", [O, fsl // 2], I32, kind="ExternalInput")
            ctx.dram(f"scsh_{wname}", [O, fsl // 32], F32, kind="ExternalInput")
        else:
            ctx.dram(f"wsh_{wname}", [O, fsl], F32, kind="ExternalInput")
        ctx.dram(f"wtsh_{wname}", [fsl, O], BF16, kind="Internal")
        ctx.dram(f"wt_{wname}", [F, O], BF16, kind="Internal")
    ctx.dram("gam_in", [128, 4], F32, kind="Internal")
    ctx.dram("gam_out", [128, 4], F32, kind="Internal")
    ctx.dram("x", [2048, 2048], F32, kind="ExternalInput")
    for bn in ("bq", "bk", "bv", "bg", "bo0", "bo1"):
        ctx.dram(bn, [1, D], F32, kind="ExternalInput")
    ctx.dram("theta", [1, 64], F32, kind="ExternalInput")
    declare_consts(ctx, ["c_iden", "c_invN", "c_rsin", "c_rcos", "c_pos1",
                         "c_flagcols", "c_flags0", "c_tri"])
    ctx.dram("xbf", [2048, 2048], BF16)
    ctx.dram("q0fm", [2048, 2048], BF16)
    ctx.dram("k0tok", [2048, 2048], BF16)
    ctx.dram("v0tok", [2048, 2048], BF16)
    ctx.dram("g0fm", [2048, 2048], F32)
    ctx.dram("stot", [4, H * 128 * 129], F32)
    ctx.dram("stot_all", [16, H * 128 * 129], F32)
    ctx.dram("x1", [2048, 2048], F32)
    ctx.dram("x1bf", [2048, 2048], BF16)
    ctx.dram("x2", [2048, 2048], F32)
    ctx.dram("x2bf", [2048, 2048], BF16)
    ctx.dram("q1tok", [2048, 2048], BF16)
    ctx.dram("k1tok", [2048, 2048], BF16)
    ctx.dram("v1tok", [2048, 2048], BF16)
    ctx.dram("kall", [8192, 2048], BF16)
    ctx.dram("vall", [8192, 2048], BF16)
    ctx.dram("x3", [2048, 2048], F32)
    ctx.dram("x3bf", [2048, 2048], BF16)
    ctx.dram("out", [2048, 2048], F32, kind="ExternalOutput")

    groups4 = [[0, 1, 2, 3], [4, 5, 6, 7]]
    emit_prep(ctx, fused=True)
    ctx.drain_dmas(); ctx.flush()
    emit_p1(ctx)
    ctx.drain_dmas(); ctx.flush()
    ctx.op("pl", lambda e: e.collective_compute(
        "AllGather", ALU.bypass, replica_groups=groups4,
        ins=[ctx.t["stot"].ap_.opt()], outs=[ctx.t["stot_all"].ap_.opt()]),
        reads=[ctx.t["stot"]], writes=[ctx.t["stot_all"]], sem="cc_stot")
    ctx.flush()
    emit_p3(ctx)
    ctx.drain_dmas(); ctx.flush()
    emit_ffn(ctx, "p4", ctx.t["x1"], ctx.t["x1bf"], ctx.t["x2"], ctx.t["x2bf"],
             ctx.t["wt_w1"], ctx.t["wt_w2"], ctx.t["wt_w3"], FH0, "sig")
    ctx.drain_dmas(); ctx.flush()
    emit_p5(ctx)
    ctx.drain_dmas(); ctx.flush()
    ctx.op("pl", lambda e: e.collective_compute(
        "AllGather", ALU.bypass, replica_groups=groups4,
        ins=[ctx.t["k1tok"].ap_.opt()], outs=[ctx.t["kall"].ap_.opt()]),
        reads=[ctx.t["k1tok"]], writes=[ctx.t["kall"]], sem="cc_kall")
    ctx.op("pl", lambda e: e.collective_compute(
        "AllGather", ALU.bypass, replica_groups=groups4,
        ins=[ctx.t["v1tok"].ap_.opt()], outs=[ctx.t["vall"].ap_.opt()]),
        reads=[ctx.t["v1tok"]], writes=[ctx.t["vall"]], sem="cc_vall")
    ctx.flush()
    emit_p7(ctx)
    ctx.drain_dmas(); ctx.flush()
    emit_ffn(ctx, "p8", ctx.t["x3"], ctx.t["x3bf"], ctx.t["out"], None,
             ctx.t["wt_g1"], ctx.t["wt_u1"], ctx.t["wt_d1"], FH1, "gelu")
    ctx.final_wait("sp")
    ctx.flush()

    in_maps = []
    for c in range(NCORE):
        m = dict(x=xs[c],
                 bq=np.asarray(inputs["L0_attn_bq"]).reshape(1, -1),
                 bk=np.asarray(inputs["L0_attn_bk"]).reshape(1, -1),
                 bv=np.asarray(inputs["L0_attn_bv"]).reshape(1, -1),
                 bg=np.asarray(inputs["L0_attn_bg"]).reshape(1, -1),
                 bo0=np.asarray(inputs["L0_attn_bo"]).reshape(1, -1),
                 bo1=np.asarray(inputs["L1_attn_bo"]).reshape(1, -1),
                 theta=np.asarray(inputs["L1_attn_theta"]).reshape(1, -1))
        for wname in WNAMES:
            key, O, F, kind = WSPECS[wname]
            fsl = F // 8
            if kind == "nvfp4":
                m[f"wsh_{wname}"] = np.ascontiguousarray(np.asarray(inputs[key])[:, c * (fsl // 2):(c + 1) * (fsl // 2)])
                m[f"scsh_{wname}"] = np.ascontiguousarray(np.asarray(inputs[SCKEY[wname]])[:, c * (fsl // 32):(c + 1) * (fsl // 32)])
            else:
                m[f"wsh_{wname}"] = np.ascontiguousarray(np.asarray(inputs[key])[:, c * fsl:(c + 1) * fsl])
        cc = consts[c]
        m.update(c_iden=cc["iden"], c_invN=cc["invN"], c_rsin=cc["rope_sin"],
                 c_rcos=cc["rope_cos"], c_pos1=cc["pos1"], c_flagcols=cc["flagcols"],
                 c_flags0=cc["flags0"], c_tri=cc["tri"])
        in_maps.append(m)
    r = L.run(in_maps)
    out = np.zeros((B, T, D), np.float32)
    for c in range(NCORE):
        out[metas[c]["g"], metas[c]["tok"], :] = r[c]["out"]
    return out


def kernel(**inputs):
    return run_fused(inputs)
